# version 02
# baseline (speedup 1.0000x reference)
import numpy as np
import concourse.bass as bass
import concourse.mybir as mybir
from concourse.bass_utils import run_bass_kernel_spmd

F32 = mybir.dt.float32
BF16 = mybir.dt.bfloat16
I32 = mybir.dt.int32
AF = mybir.ActivationFunctionType
ALU = mybir.AluOpType
AX = mybir.AxisListType

ENGINES = ("pe", "dve", "act", "pool", "sp")
SECTION = {"pe": "tensor", "dve": "vector", "act": "scalar", "pool": "gpsimd", "sp": "sync"}


class Prog:
    def __init__(self, name="prog"):
        self.nc = bass.Bass("TRN2", target_bir_lowering=False, debug=False)
        self.name = name
        self.ops = {e: [] for e in ENGINES}
        self.engine_count = {e: 0 for e in ENGINES}
        self.sems = {}
        self.res_count = {}
        self.res_queue = {}
        self.mem = {}
        self.whole = set()
        self.readonly = set()
        self.free_elems = {}
        self.engine_vc = {e: {} for e in ENGINES}
        self.op_vc = {}
        for e in ENGINES:
            self.sems["e_" + e] = self.nc.alloc_semaphore("e_" + e)

    def sbuf(self, name, shape, dtype=F32):
        h = self.nc.alloc_sbuf_tensor(name, list(shape), dtype)
        self.free_elems[name] = int(np.prod(shape[1:]))
        return h.ap()

    def psum(self, name, shape, dtype=F32):
        h = self.nc.alloc_psum_tensor(name, list(shape), dtype)
        self.free_elems[name] = int(np.prod(shape[1:]))
        self.whole.add(name)
        return h.ap()

    def dram_in(self, name, shape, dtype=F32):
        ap = self.nc.dram_tensor(name, list(shape), dtype, kind="ExternalInput").ap()
        self.free_elems[name] = int(np.prod(shape))
        self.readonly.add(name)
        return ap

    def dram_out(self, name, shape, dtype=F32):
        ap = self.nc.dram_tensor(name, list(shape), dtype, kind="ExternalOutput").ap()
        self.free_elems[name] = int(np.prod(shape))
        return ap

    def dram_tmp(self, name, shape, dtype=F32):
        ap = self.nc.dram_tensor(name, list(shape), dtype, kind="Internal").ap()
        self.free_elems[name] = int(np.prod(shape))
        return ap

    def resource(self, name, queue):
        if name not in self.res_count:
            self.sems[name] = self.nc.alloc_semaphore(name)
            self.res_count[name] = 0
            self.res_queue[name] = queue
        return name

    def _region(self, ap):
        t = ap.tensor
        name = t.name
        fe = self.free_elems.get(name)
        if name in self.whole or fe is None:
            return (name, 0, 1 << 20, 0, 1 << 40)
        kind = type(t).__name__
        if "SBTensor" in kind or "PSum" in kind:
            try:
                p0 = ap.base_partition()
                psz = ap.partition_size()
                off = ap.offset - p0 * fe
                pdim = None
                for i, (st, ct) in enumerate(ap.ap):
                    if st == fe and ct == psz:
                        pdim = i
                        break
                span = 0
                for i, (st, ct) in enumerate(ap.ap):
                    if i == pdim:
                        continue
                    span += abs(st) * (ct - 1)
                if pdim is None and psz != 1:
                    return (name, 0, 1 << 20, 0, 1 << 40)
                return (name, p0, p0 + psz, off, off + span + 1)
            except Exception:
                return (name, 0, 1 << 20, 0, 1 << 40)
        off = ap.offset
        span = 0
        for (st, ct) in ap.ap:
            span += abs(st) * (ct - 1)
        return (name, 0, 1 << 20, off, off + span + 1)

    @staticmethod
    def _overlap(a, b):
        return a[1] < b[2] and b[1] < a[2] and a[3] < b[4] and b[3] < a[4]

    @staticmethod
    def _covers(a, b):
        return a[1] <= b[1] and a[2] >= b[2] and a[3] <= b[3] and a[4] >= b[4]

    def _track(self, reads, writes):
        deps = []
        rregions = []
        wregions = [self._region(a) for a in writes]
        for a in reads:
            if a.tensor.name in self.readonly:
                continue
            if a.tensor.name in self.whole:
                wregions.append(self._region(a))
            else:
                rregions.append(self._region(a))
        for r in rregions:
            for (reg, iswrite, tag) in self.mem.get(r[0], ()):
                if iswrite and self._overlap(reg, r):
                    deps.append(tag)
        for w in wregions:
            for (reg, iswrite, tag) in self.mem.get(w[0], ()):
                if self._overlap(reg, w):
                    deps.append(tag)
        deps = [(s, v) if s.startswith("e_") else (s, self.res_count[s])
                for (s, v) in deps]
        return rregions, wregions, deps

    def _commit(self, rregions, wregions, tag):
        for r in rregions:
            self.mem.setdefault(r[0], []).append((r, False, tag))
        for w in wregions:
            recs = self.mem.setdefault(w[0], [])
            recs[:] = [rec for rec in recs if not self._covers(w, rec[0])]
            recs.append((w, True, tag))

    def _reduce_waits(self, engine, deps):
        own = "e_" + engine
        vc = self.engine_vc.setdefault(engine, {})
        waits = []
        for (s, v) in sorted(set(deps), key=lambda d: (d[0], -d[1])):
            if vc.get(s, 0) >= v:
                continue
            waits.append((s, v))
            vc[s] = v
            pvc = self.op_vc.get((s, v))
            if pvc:
                for (s2, v2) in pvc.items():
                    if s2 == own:
                        continue
                    if vc.get(s2, 0) < v2:
                        vc[s2] = v2
        return waits

    def op(self, engine, emit, reads=(), writes=(), extra_deps=()):
        rr, wr, deps = self._track(list(reads), list(writes))
        deps = deps + list(extra_deps)
        if engine == "pe":
            deps = [d for d in deps if d[0] != "e_pe"]
        waits = self._reduce_waits(engine, deps)
        self.engine_count[engine] += 1
        tag = ("e_" + engine, self.engine_count[engine])
        snap = dict(self.engine_vc[engine])
        snap[tag[0]] = tag[1]
        self.op_vc[tag] = snap
        self.ops[engine].append(dict(emit=emit, waits=waits, sem="e_" + engine, inc=1))
        self._commit(rr, wr, tag)
        return tag

    def dma(self, queue, out, in_, res, slow=False, batch=False, extra_deps=()):
        self.resource(res, queue)
        rr, wr, deps = self._track([in_], [out])
        deps = deps + list(extra_deps)
        waits = self._reduce_waits(queue, deps)
        if self.res_count[res] > 0:
            w = (res, self.res_count[res])
            if w not in waits:
                waits = waits + [w]
            vc = self.engine_vc[queue]
            if vc.get(res, 0) < w[1]:
                vc[res] = w[1]
        self.res_count[res] += 16
        tag = (res, self.res_count[res])
        snap = dict(self.engine_vc[queue])
        snap[res] = self.res_count[res]
        self.op_vc[tag] = snap

        def emit(eng, out=out, in_=in_, slow=slow):
            if slow:
                return eng.dma_start(out=out, in_=in_, allow_slow_non_contiguous=True)
            return eng.dma_start(out=out, in_=in_)
        self.ops[queue].append(dict(emit=emit, waits=waits, sem=res, inc=16))
        self._commit(rr, wr, tag)
        return tag

    def finalize(self):
        nc = self.nc
        stats = {e: len(self.ops[e]) for e in ENGINES}
        stats["waits"] = 0
        with nc.Block() as block:
            for engine in ENGINES:
                ops = self.ops[engine]
                final_res = [(r, c) for r, c in self.res_count.items()
                             if c > 0 and self.res_queue[r] == engine and r.startswith("out")]
                if not ops and not final_res:
                    continue
                section = getattr(block, SECTION[engine])

                def body(eng, ops=ops, engine=engine, final_res=final_res):
                    for o in ops:
                        for (s, v) in o["waits"]:
                            eng.wait_ge(self.sems[s], v)
                            stats["waits"] += 1
                        inst = o["emit"](eng)
                        inst.then_inc(self.sems[o["sem"]], o["inc"])
                    for (r, c) in final_res:
                        eng.wait_ge(self.sems[r], c)
                section(body)
        self.stats = stats
        return nc

    def tt(self, engine, out, a, b, alu):
        return self.op(engine, lambda e: e.tensor_tensor(out, a, b, alu),
                       reads=[a, b], writes=[out])

    def ts(self, engine, out, in0, s1, s2, op0, op1=ALU.bypass):
        reads = [in0] + [x for x in (s1, s2) if hasattr(x, "tensor")]
        return self.op(engine, lambda e: e.tensor_scalar(out, in0, s1, s2, op0, op1),
                       reads=reads, writes=[out])

    def stt(self, engine, out, in0, scalar, in1, op0, op1):
        engine = "dve"
        reads = [in0, in1] + ([scalar] if hasattr(scalar, "tensor") else [])
        return self.op(engine, lambda e: e.scalar_tensor_tensor(out, in0, scalar, in1, op0, op1),
                       reads=reads, writes=[out])

    def copy(self, engine, out, in_):
        if engine == "act":
            return self.op("act", lambda e: e.copy(out, in_), reads=[in_], writes=[out])
        return self.op(engine, lambda e: e.tensor_copy(out, in_), reads=[in_], writes=[out])

    def memset(self, engine, ap, val):
        return self.op(engine, lambda e: e.memset(ap, val), reads=[], writes=[ap])

    def act(self, out, in_, func, bias=0.0, scale=1.0, accum_out=None):
        reads = [in_] + [x for x in (bias, scale) if hasattr(x, "tensor")]
        writes = [out] + ([accum_out] if accum_out is not None else [])
        return self.op("act", lambda e: e.activation(out, in_, func, bias=bias, scale=scale,
                                                     accum_out=accum_out),
                       reads=reads, writes=writes)

    def rsum(self, out, in_, engine="dve"):
        return self.op(engine, lambda e: e.reduce_sum(out, in_, AX.X),
                       reads=[in_], writes=[out])

    def rmax(self, out, in_, absval=False, engine="dve"):
        return self.op(engine, lambda e: e.reduce_max(out, in_, AX.X, apply_absolute_value=absval),
                       reads=[in_], writes=[out])

    def scan(self, out, d0, d1, init, op0, op1, engine="dve"):
        return self.op(engine, lambda e: e.tensor_tensor_scan(out, d0, d1, init, op0, op1),
                       reads=[d0, d1], writes=[out])

    def mm(self, out, lhsT, rhs, start=True, stop=True):
        return self.op("pe", lambda e: e.matmul(out, lhsT, rhs, start=start, stop=stop),
                       reads=[lhsT, rhs], writes=[out])

    def tp(self, out, in_, ident):
        return self.op("pe", lambda e: e.transpose(out, in_, ident),
                       reads=[in_, ident], writes=[out])


class PsumPool:
    def __init__(self, b):
        self.banks = [b.psum("pbank%d" % i, [128, 512], F32) for i in range(8)]
        self.i = 0
        self.reserved = 0

    def take(self):
        for _ in range(8):
            j = self.i % 8
            self.i += 1
            if j >= self.reserved:
                return self.banks[j]
        raise RuntimeError("no free banks")

    def fixed(self, j):
        return self.banks[j]


def host_consts():
    ii = np.arange(128)
    c = {}
    c["c_iden"] = np.eye(128, dtype=np.float32)
    c["c_iden2"] = np.vstack([np.eye(64, dtype=np.float32)] * 2)
    c["c_ones_row"] = np.ones((1, 128), dtype=np.float32)
    c["c_ones_col"] = np.ones((128, 1), dtype=np.float32)
    c["c_imj"] = (ii[None, :] - ii[:, None]).astype(np.float32)
    c["c_trilneg"] = np.where(ii[:, None] <= ii[None, :], 0.0, -3e38).astype(np.float32)
    c["c_ip1"] = np.tile((ii + 1).astype(np.float32)[None, :], (128, 1))
    c["c_rev"] = (127.0 - ii).astype(np.float32)[:, None]
    return c


CONST_SPECS = {k: v.shape for k, v in host_consts().items()}


class Ctx:
    pass


def make_ctx(b, T, blocks):
    cx = Ctx()
    cx.T = T
    S = T // 128
    cx.S = S
    cx.pool = PsumPool(b)
    cx.iden = b.sbuf("iden", [128, 128])
    cx.iden2 = b.sbuf("iden2", [128, 64])
    cx.iden_bf = b.sbuf("iden_bf", [128, 128], BF16)
    cx.iden2_bf = b.sbuf("iden2_bf", [128, 64], BF16)
    cx.ones_row = b.sbuf("ones_row", [1, 128])
    cx.ones_col = b.sbuf("ones_col", [128, 1])
    cx.imj = b.sbuf("imj", [128, 128])
    cx.trilneg = b.sbuf("trilneg", [128, 128])
    cx.ip1 = b.sbuf("ip1", [128, 128])
    cx.rev = b.sbuf("rev", [128, 1])
    cx.km = b.sbuf("km", [128, 8])
    cx.X = b.sbuf("X", [128, S, 1024])
    cx.LN = b.sbuf("LN", [128, S, 1024])
    cx.SCR = b.sbuf("SCR", [128, 1024])
    cx.SCR2 = b.sbuf("SCR2", [128, 1024])
    cx.TMA = b.sbuf("TMA", [128, S, 2048], BF16)
    cx.TMB = b.sbuf("TMB", [128, max(S, 2) * 2048], BF16)
    cx.FM1 = b.sbuf("FM1", [128, 8, T], BF16)
    cx.FMW = b.sbuf("FMW", [128, 16, T], BF16)
    cx.MID = b.sbuf("MID", [128, 32, T], BF16)
    cx.cols = b.sbuf("cols", [128, 20])
    cx.QS = b.sbuf("QS", [128, 1024])
    cx.ring = [b.sbuf("wslot%d" % i, [128, 512], BF16) for i in range(5)]
    cx.ring_i = 0
    cx.As = b.sbuf("As", [128, 128])
    cx.Ms = b.sbuf("Ms", [128, 128])
    cx.D1s = b.sbuf("D1s", [128, 128])
    cx.AMs = b.sbuf("AMs", [128, 128], BF16)
    cx.Ktok = b.sbuf("Ktok", [128, 64], BF16)
    cx.Vtok = b.sbuf("Vtok", [128, 64], BF16)
    cx.KDs = b.sbuf("KDs", [128, 64], BF16)
    cx.QGs = b.sbuf("QGs", [128, 128], BF16)
    cx.utok = b.sbuf("utok", [128, 64], BF16)
    cx.xht = b.sbuf("xht", [128, 64], BF16)
    cx.Btok = b.sbuf("Btok", [128, 128], BF16)
    cx.ytmp = b.sbuf("ytmp", [128, 64])
    cx.rowbuf = b.sbuf("rowbuf", [1, 128])
    cx.ccol = b.sbuf("ccol", [128, 4])
    if "l0a" in blocks:
        cx.STR = b.sbuf("STR", [128, 8 * 64], BF16)
        cx.DMASK = b.sbuf("DMASK", [128, 16 * 128], BF16)
        cx.G1 = b.sbuf("G1", [128, 16 * 128], BF16)
        cx.DECAYK = b.sbuf("DECAYK", [128, 16])
        cx.GC = b.sbuf("GC", [128, 16])
        cx.LNG = b.sbuf("LNG", [128, 16])
        cx.qkvb = b.sbuf("qkvb", [128, 24])
        cx.outb = b.sbuf("outb", [128, 8])
    if "l1a" in blocks:
        cx.CONV = b.sbuf("CONV", [128, 18, T + 4], BF16)
        cx.XC = b.sbuf("XC", [128, 18, T], BF16)
        cx.STM = b.sbuf("STM", [128, 32 * 64], BF16)
        cx.convw = b.sbuf("convw", [128, 18, 4])
        cx.convb = b.sbuf("convb", [128, 18])
        cx.dtb = b.sbuf("dtb", [32, 1])
        cx.nega = b.sbuf("nega", [32, 1])
        cx.DSKC = b.sbuf("DSKC", [128, 32])
        cx.dts = b.sbuf("dts", [32, 4 * T])
        cx.acum = cx.dts[:, 2 * T:3 * T]
        cx.a_s = cx.dts[:, 1 * T:2 * T]
        cx.acumT = b.sbuf("acumT", [128, 32])
        cx.dtT = b.sbuf("dtT", [128, 32])
        cx.zeros32 = b.sbuf("zeros32", [32, 1])
        cx.ROWS = b.sbuf("ROWS", [1, 1024])
        cx.NWC = b.sbuf("NWC", [128, 16])
    if "l1f" in blocks:
        if not hasattr(cx, "ROWS"):
            cx.ROWS = b.sbuf("ROWS", [1, 1024])
        cx.LNWc = b.sbuf("LNWc", [128, 8])
        cx.LNBc = b.sbuf("LNBc", [128, 8])
        cx.SCB = b.sbuf("SCB", [128, T], BF16)
        cx.WBs = cx.SCB
        cx.RW = b.sbuf("RW", [128, 8, 8], BF16)
        cx.LG = b.sbuf("LG", [128, S, 8])
        cx.WCOL = b.sbuf("WCOL", [128, S, 8])
        cx.wrow = cx.ROWS[0:1, 0:T] if hasattr(cx, "ROWS") else b.sbuf("wrow", [1, T])
    return cx


def ring_next(cx):
    s = cx.ring[cx.ring_i % 5]
    cx.ring_i += 1
    return s, "wr%d" % ((cx.ring_i - 1) % 5)



def emit_consts(b, cx, consts_aps):
    for name, ap in consts_aps.items():
        dst = getattr(cx, name[2:])
        b.dma("sp", dst, ap, "ld_const", batch=True)
    vals = [1e-5, -0.5, 1.0, 1.1920929e-07, 128.0, 0.5, float(np.log(1.0 / 8.0)), 1e-5]
    for i, v in enumerate(vals):
        b.memset("dve", cx.km[:, i:i + 1], v)
    b.copy("dve", cx.iden_bf, cx.iden)
    b.copy("dve", cx.iden2_bf, cx.iden2)


def ln_stats_apply(b, cx, dst, src, eps_col, do_mean=True):
    c = cx.cols
    mean, msq, var, rstd = c[:, 0:1], c[:, 1:2], c[:, 2:3], c[:, 3:4]
    n = src.shape[-1]
    if do_mean:
        b.rsum(mean, src)
        b.ts("dve", mean, mean, 1.0 / n, None, ALU.mult)
    b.act(cx.SCR2[:, 0:n], src, AF.Square, accum_out=msq)
    b.ts("dve", msq, msq, 1.0 / n, None, ALU.mult)
    if do_mean:
        b.tt("dve", var, mean, mean, ALU.mult)
        b.tt("dve", var, msq, var, ALU.subtract)
    else:
        b.copy("dve", var, msq)
    b.act(rstd, var, AF.Ln, bias=eps_col)
    b.act(rstd, rstd, AF.Exp, scale=cx.km[:, 1:2])
    if do_mean:
        b.ts("dve", dst, src, mean, rstd, ALU.subtract, ALU.mult)
    else:
        b.ts("dve", dst, src, rstd, None, ALU.mult)


def tp128(b, cx, pairs, add=False, evac="dve"):
    i = 0
    while i < len(pairs):
        grp = pairs[i:i + 4]
        bank = cx.pool.take()
        views = []
        for j, (s, d) in enumerate(grp):
            if s.dtype == BF16:
                pa = bank.bitcast(BF16)[:, j * 256:j * 256 + 128]
                b.tp(pa, s, cx.iden_bf)
            else:
                pa = bank[:, j * 128:(j + 1) * 128]
                b.tp(pa, s, cx.iden)
            views.append(pa)
        for j, (s, d) in enumerate(grp):
            if add:
                b.tt(evac, d, views[j], d, ALU.add)
            else:
                b.copy(evac, d, views[j])
        i += 4


def tm_to_fm(b, cx, fm, tm, nf):
    pairs = []
    for s in range(cx.S):
        for f in range(nf):
            pairs.append((tm[:, s, f * 128:(f + 1) * 128],
                          fm[:, f, s * 128:(s + 1) * 128]))
    tp128(b, cx, pairs)


def stream_mm(b, cx, wsrc, K, M, rhs_fn, evac_fn):
    T = cx.T
    nk = (K + 127) // 128
    mtiles = []
    m0 = 0
    while m0 < M:
        msz = min(128, M - m0)
        mtiles.append((m0, msz))
        m0 += msz
    gi = 0
    while gi < len(mtiles):
        grp = mtiles[gi:gi + 4]
        paps = [cx.pool.take()[0:msz, 0:T] for (m0, msz) in grp]
        for k in range(nk):
            ksz = min(128, K - k * 128)
            wcols = sum(msz for (_, msz) in grp)
            slot, res = ring_next(cx)
            b.dma("sp", slot[0:ksz, 0:wcols],
                  wsrc[k * 128:k * 128 + ksz, grp[0][0]:grp[0][0] + wcols], res)
            co = 0
            for j, (m0, msz) in enumerate(grp):
                b.mm(paps[j], slot[0:ksz, co:co + msz], rhs_fn(k)[0:ksz, :],
                     start=(k == 0), stop=(k == nk - 1))
                co += msz
        for j, (m0, msz) in enumerate(grp):
            evac_fn(gi + j, paps[j])
        gi += 4



L0A_SETUP = 99

def emit_l0a_setup(b, cx, W):
    if globals()["L0A_SETUP"] < 1:
        return
    b.dma("sp", cx.qkvb, W["L0_attn_Wqkv_b"].rearrange("(t p) -> p t", p=128), "ld_small", slow=True, batch=True)
    b.dma("sp", cx.outb, W["L0_attn_out_b"].rearrange("(t p) -> p t", p=128), "ld_small", slow=True, batch=True)
    grow = cx.rowbuf
    b.dma("sp", grow[0:1, 0:16], W["L0_attn_gamma_raw"].unsqueeze(0), "ld_small", batch=True)
    t = cx.ccol
    r0 = cx.SCR[0:1, 0:16]
    r1 = cx.SCR[0:1, 16:32]
    r2 = cx.SCR[0:1, 32:48]
    b.act(r0, grow[0:1, 0:16], AF.Abs)
    b.act(r0, r0, AF.Exp, scale=-1.0)
    b.act(r0, r0, AF.Ln, bias=1.0)
    b.ts("dve", r1, grow[0:1, 0:16], -1.0, 0.0, ALU.mult, ALU.max)
    b.tt("dve", r2, r1, r0, ALU.add)
    b.ts("dve", r2, r2, -1.0, None, ALU.mult)
    bank = cx.pool.take()
    b.mm(bank[:, 0:16], cx.ones_row, r2)
    b.copy("dve", cx.LNG, bank[:, 0:16])
    for h in range(16):
        lc = cx.LNG[:, h:h + 1]
        dm = cx.DMASK[:, h * 128:(h + 1) * 128]
        b.ts("dve", dm, cx.imj, lc, cx.km[:, 6:7], ALU.mult, ALU.add)
        b.tt("pool", dm, dm, cx.trilneg, ALU.add)
        b.act(dm, dm, AF.Exp)
        g1 = cx.G1[:, h * 128:(h + 1) * 128]
        b.ts("dve", g1, cx.ip1, lc, None, ALU.mult)
        b.act(g1, g1, AF.Exp)
        b.ts("dve", cx.DECAYK[:, h:h + 1], cx.rev, lc, cx.km[:, 6:7], ALU.mult, ALU.add)
        b.act(cx.DECAYK[:, h:h + 1], cx.DECAYK[:, h:h + 1], AF.Exp)
        b.act(cx.GC[:, h:h + 1], lc, AF.Exp, scale=cx.km[:, 4:5])
    if globals()["L0A_SETUP"] < 2:
        return
    emit_transpose_prep(b, cx, W["L0_attn_Wqkv_w"], b.scr_wqkvT, 3072, 1024)
    emit_transpose_prep(b, cx, W["L0_attn_out_w"], b.scr_outT, 1024, 1024)


def emit_transpose_prep(b, cx, wsrc, wdst, O, I, quant=None):
    CC = min(I, 1024)
    for r in range(O // 128):
        for cc in range(I // CC):
            b.dma("sp", cx.SCR[:, 0:CC], wsrc[r * 128:(r + 1) * 128, cc * CC:(cc + 1) * CC],
                  "ld_prep")
            src = cx.SCR[:, 0:CC]
            if quant is not None:
                scol, thrcol = quant
                b.act(cx.SCR2, src, AF.Sign)
                b.act(cx.QS, src, AF.Abs)
                b.ts("dve", cx.QS, cx.QS, thrcol, None, ALU.is_ge)
                b.tt("pool", cx.SCR2, cx.SCR2, cx.QS, ALU.mult)
                b.ts("dve", cx.SCR2, cx.SCR2, scol, None, ALU.mult)
                src = cx.SCR2[:, 0:CC]
            pairs = []
            dst = cx.TMB[:, 4096:4096 + CC]
            for cb in range(CC // 128):
                pairs.append((src[:, cb * 128:(cb + 1) * 128], dst[:, cb * 128:(cb + 1) * 128]))
            tp128(b, cx, pairs)
            b.dma("sp", wdst.rearrange("(c p) o -> p c o", p=128)[:, cc * (CC // 128):(cc + 1) * (CC // 128), r * 128:(r + 1) * 128],
                  dst.rearrange("p (c m) -> p c m", m=128), "st_prep", batch=True)


def emit_bf16_copy(b, cx, wsrc, wdst, R, C):
    for r in range((R + 127) // 128):
        rs = min(128, R - r * 128)
        b.dma("pool", cx.TMB[0:rs, 0:C], wsrc[r * 128:r * 128 + rs, :], "ld_cast")
        b.dma("sp", wdst[r * 128:r * 128 + rs, :], cx.TMB[0:rs, 0:C], "st_prep", batch=True)


def emit_wmean_scale(b, cx, wsrc, O, I, scol, thrcol):
    CC = min(I, 1024)
    bank0 = cx.pool.take()
    bank1 = cx.pool.take()
    nblk = (O // 128) * (I // CC)
    i = 0
    for r in range(O // 128):
        for cc in range(I // CC):
            b.dma("sp", cx.SCR[:, 0:CC], wsrc[r * 128:(r + 1) * 128, cc * CC:(cc + 1) * CC],
                  "ld_prep")
            b.act(cx.SCR2, cx.SCR[:, 0:CC], AF.Abs)
            b.mm(bank0[0:1, 0:512], cx.ones_col, cx.SCR2[:, 0:512],
                 start=(i == 0), stop=(i == nblk - 1))
            b.mm(bank1[0:1, 0:512], cx.ones_col, cx.SCR2[:, 512:1024],
                 start=(i == 0), stop=(i == nblk - 1))
            i += 1
    rows = cx.SCR[0:1, 0:1024]
    b.copy("dve", rows[0:1, 0:512], bank0[0:1, 0:512])
    b.copy("dve", rows[0:1, 512:1024], bank1[0:1, 0:512])
    b.rsum(cx.QS[0:1, 0:1], rows)
    srow = cx.QS[0:1, 2:4]
    b.ts("dve", srow[0:1, 0:1], cx.QS[0:1, 0:1], 1.0 / (O * I), 1e-5,
         ALU.mult, ALU.max)
    b.ts("dve", srow[0:1, 1:2], srow[0:1, 0:1], 0.5, None, ALU.mult)
    bank = cx.pool.take()
    b.mm(bank[:, 0:2], cx.ones_row, srow)
    b.copy("dve", scol, bank[:, 0:1])
    b.copy("dve", thrcol, bank[:, 1:2])


def quant_act_tm(b, cx, strips, full_row_ap):
    c = cx.cols
    amax, rec = c[:, 4:5], c[:, 5:6]
    b.rmax(amax, full_row_ap, absval=True)
    b.ts("dve", amax, amax, 1e-5, None, ALU.max)
    b.op("dve", lambda e: e.reciprocal(rec, amax), reads=[amax], writes=[rec])
    for st in strips:
        n = st.shape[-1]
        v = cx.SCR[:, 0:n]
        b.ts("dve", v, st, rec, 127.0, ALU.mult, ALU.mult)
        i32 = cx.SCR2.bitcast(I32)[:, 0:n]
        b.copy("dve", i32, v)
        b.copy("dve", st, i32)
        f = cx.SCR2[:, 0:n]
        b.tt("dve", f, v, st, ALU.subtract)
        g1 = cx.QS[:, 0:n]
        b.ts("pool", g1, f, 0.5, None, ALU.is_ge)
        b.ts("dve", f, f, -0.5, None, ALU.is_le)
        b.tt("dve", st, st, g1, ALU.add)
        b.tt("pool", st, st, f, ALU.subtract)
        b.ts("dve", st, st, -128.0, 127.0, ALU.max, ALU.min)
        b.ts("dve", st, st, amax, 1.0 / 127.0, ALU.mult, ALU.mult)


L0A_DEPTH = 99
L0A_CH = 99

def emit_l0a_tile(b, cx, first, W):
    T, S = cx.T, cx.S
    if first:
        b.memset("dve", cx.STR, 0.0)
    if globals()["L0A_DEPTH"] < 1:
        return
    for s in range(S):
        ln_stats_apply(b, cx, cx.LN[:, s, :], cx.X[:, s, :], cx.km[:, 0:1])
    if globals()["L0A_DEPTH"] < 2:
        return
    tm_to_fm(b, cx, cx.FM1, cx.LN, 8)
    def evac_qkv(mt, pap):
        b.act(cx.MID[:, mt, :], pap, AF.Identity, bias=cx.qkvb[:, mt:mt + 1])
    stream_mm(b, cx, b.scr_wqkvT, 1024, 3072, lambda k: cx.FM1[:, k, :], evac_qkv)
    if globals()["L0A_DEPTH"] < 3:
        return
    for c in range(S):
        cc = slice(c * 128, (c + 1) * 128)
        for h in range(16):
            hb = (h % 2) * 64
            q = cx.MID[hb:hb + 64, h // 2, cc]
            kk = cx.MID[hb:hb + 64, 8 + h // 2, cc]
            v = cx.MID[hb:hb + 64, 16 + h // 2, cc]
            Sst = cx.STR[hb:hb + 64, (h // 2) * 64:(h // 2) * 64 + 64]
            CH = globals()["L0A_CH"]
            bank = cx.pool.take()
            b.mm(bank[:, 0:128], kk, q)
            b.tt("dve", cx.AMs, bank[:, 0:128], cx.DMASK[:, h * 128:(h + 1) * 128], ALU.mult)
            if CH < 2:
                b.copy("dve", cx.TMA[:, c, h * 64:(h + 1) * 64], cx.AMs[:, 0:64])
                continue
            bank2 = cx.pool.take().bitcast(BF16)
            b.tp(bank2[:, 0:64], kk, cx.iden2_bf[hb:hb + 64, :])
            b.tp(bank2[:, 256:320], v, cx.iden2_bf[hb:hb + 64, :])
            b.copy("dve", cx.Ktok, bank2[:, 0:64])
            b.copy("act", cx.Vtok, bank2[:, 256:320])
            b.ts("pool", cx.KDs, cx.Ktok, cx.DECAYK[:, h:h + 1], None, ALU.mult)
            b.tt("pool", cx.QGs[hb:hb + 64, :], q, cx.G1[hb:hb + 64, h * 128:(h + 1) * 128], ALU.mult)
            if CH < 3:
                b.copy("dve", cx.TMA[:, c, h * 64:(h + 1) * 64], cx.KDs)
                continue
            bankY = cx.pool.take()
            b.mm(bankY[:, 0:64], cx.AMs, cx.Vtok)
            b.mm(bankY[:, 64:128], cx.QGs[hb:hb + 64, :], Sst)
            ytm = cx.TMA[:, c, h * 64:(h + 1) * 64]
            b.copy("dve", ytm, bankY[:, 0:64])
            b.tt("dve", ytm, bankY[:, 64:128], ytm, ALU.add)
            if CH < 4:
                continue
            bankS = cx.pool.take()
            b.mm(bankS[hb:hb + 64, 0:64], cx.KDs, cx.Vtok)
            b.stt("dve", Sst, Sst, cx.GC[hb:hb + 64, h:h + 1], bankS[hb:hb + 64, 0:64],
                  ALU.mult, ALU.add)
    if globals()["L0A_DEPTH"] < 4:
        return
    pairs = []
    for s in range(S):
        for f in range(8):
            pairs.append((cx.TMA[:, s, f * 128:(f + 1) * 128],
                          cx.FM1[:, f, s * 128:(s + 1) * 128]))
    tp128(b, cx, pairs)
    def evac_out(mt, pap):
        b.act(cx.FMW[:, mt, :], pap, AF.Identity, bias=cx.outb[:, mt:mt + 1])
    stream_mm(b, cx, b.scr_outT, 1024, 1024, lambda k: cx.FM1[:, k, :], evac_out)
    pairs = []
    for s in range(S):
        for f in range(8):
            pairs.append((cx.FMW[:, f, s * 128:(s + 1) * 128],
                          cx.X[:, s, f * 128:(f + 1) * 128]))
    tp128(b, cx, pairs, add=True)





def emit_l0f_setup(b, cx, W):
    emit_wmean_scale(b, cx, W["L0_ffn_w1"], 4096, 1024, cx.cols[:, 8:9], cx.cols[:, 9:10])
    emit_transpose_prep(b, cx, W["L0_ffn_w1"], b.scr_w1qT, 4096, 1024,
                        quant=(cx.cols[:, 8:9], cx.cols[:, 9:10]))
    emit_wmean_scale(b, cx, W["L0_ffn_w2"], 1024, 4096, cx.cols[:, 10:11], cx.cols[:, 11:12])
    emit_transpose_prep(b, cx, W["L0_ffn_w2"], b.scr_w2qT, 1024, 4096,
                        quant=(cx.cols[:, 10:11], cx.cols[:, 11:12]))


def emit_l0f_tile(b, cx, W):
    T, S = cx.T, cx.S
    for s in range(S):
        ln_stats_apply(b, cx, cx.LN[:, s, :], cx.X[:, s, :], cx.km[:, 0:1])
        quant_act_tm(b, cx, [cx.LN[:, s, :]], cx.LN[:, s, :])
    tm_to_fm(b, cx, cx.FM1, cx.LN, 8)

    def evac_h(mt, pap):
        b.act(cx.MID[:, mt, :], pap, AF.Gelu)
    stream_mm(b, cx, b.scr_w1qT, 1024, 4096, lambda k: cx.FM1[:, k, :], evac_h)
    for s in range(S):
        pairs = [(cx.MID[:, m, s * 128:(s + 1) * 128], cx.TMB[:, m * 128:(m + 1) * 128])
                 for m in range(32)]
        tp128(b, cx, pairs)
        quant_act_tm(b, cx, [cx.TMB[:, j * 1024:(j + 1) * 1024] for j in range(4)],
                     cx.TMB[:, 0:4096])
        pairs = [(cx.TMB[:, m * 128:(m + 1) * 128], cx.MID[:, m, s * 128:(s + 1) * 128])
                 for m in range(32)]
        tp128(b, cx, pairs)

    def evac_o(mt, pap):
        b.copy("dve", cx.FMW[:, mt, :], pap)
    stream_mm(b, cx, b.scr_w2qT, 4096, 1024, lambda k: cx.MID[:, k, :], evac_o)
    pairs = []
    for s in range(S):
        for f in range(8):
            pairs.append((cx.FMW[:, f, s * 128:(s + 1) * 128],
                          cx.X[:, s, f * 128:(f + 1) * 128]))
    tp128(b, cx, pairs, add=True)




def tp_cols(b, cx, pairs):
    i = 0
    while i < len(pairs):
        grp = pairs[i:i + 4]
        bank = cx.pool.take()
        for j, (r, d) in enumerate(grp):
            b.tp(bank[0:d.shape[0], j:j + 1], r, cx.ones_row[0:1, 0:1])
        for j, (r, d) in enumerate(grp):
            b.copy("dve", d, bank[0:d.shape[0], j:j + 1])
        i += 4


def emit_l1a_setup(b, cx, W):
    cwflat = W["L1_attn_conv_w"].rearrange("a b -> (a b)").unsqueeze(0)
    for t in range(18):
        b.dma("sp", cx.ROWS[0:1, 0:512], cwflat[0:1, t * 512:(t + 1) * 512], "ld_small", batch=True)
        RV = cx.ROWS[0:1, 0:512].rearrange("r (p k) -> r p k", k=4)
        pairs = [(RV[0:1, :, k], cx.convw[:, t, k:k + 1]) for k in range(4)]
        tp_cols(b, cx, pairs)
    for ch in range(3):
        n = min(1024, 2304 - ch * 1024)
        b.dma("sp", cx.ROWS[0:1, 0:n], W["L1_attn_conv_b"][ch * 1024:ch * 1024 + n].unsqueeze(0),
              "ld_small", batch=True)
        tp_cols(b, cx, [(cx.ROWS[0:1, i * 128:(i + 1) * 128], cx.convb[:, ch * 8 + i:ch * 8 + i + 1])
                        for i in range(n // 128)])
    b.dma("sp", cx.ROWS[0:1, 0:32], W["L1_attn_dt_bias"].unsqueeze(0), "ld_small", batch=True)
    b.dma("sp", cx.ROWS[0:1, 64:96], W["L1_attn_a_log"].unsqueeze(0), "ld_small", batch=True)
    b.dma("sp", cx.ROWS[0:1, 128:160], W["L1_attn_d_skip"].unsqueeze(0), "ld_small", batch=True)
    tp_cols(b, cx, [(cx.ROWS[0:1, 0:32], cx.dtb),
                    (cx.ROWS[0:1, 64:96], cx.nega)])
    b.act(cx.nega, cx.nega, AF.Exp)
    b.ts("dve", cx.nega, cx.nega, -1.0, None, ALU.mult)
    bank = cx.pool.take()
    b.mm(bank[:, 0:32], cx.ones_row, cx.ROWS[0:1, 128:160])
    b.copy("dve", cx.DSKC, bank[:, 0:32])
    b.memset("dve", cx.zeros32, 0.0)
    for ch in range(2):
        b.dma("sp", cx.ROWS[0:1, 0:1024], W["L1_attn_norm_w"][ch * 1024:(ch + 1) * 1024].unsqueeze(0),
              "ld_small", batch=True)
        tp_cols(b, cx, [(cx.ROWS[0:1, i * 128:(i + 1) * 128], cx.NWC[:, ch * 8 + i:ch * 8 + i + 1])
                        for i in range(8)])
    for r in range(16):
        b.dma("sp", cx.SCR, W["L1_attn_w_out"][r * 128:(r + 1) * 128, :], "ld_prep")
        b.ts("dve", cx.TMB[:, 0:1024], cx.SCR, cx.NWC[:, r:r + 1], None, ALU.mult)
        b.dma("sp", b.scr_woutF[r * 128:(r + 1) * 128, :], cx.TMB[:, 0:1024], "st_prep", batch=True)
    emit_bf16_copy(b, cx, W["L1_attn_w_in"], b.scr_w_in, 1024, 4384)
    emit_bf16_copy(b, cx, W["L1_attn_w_fc"], b.scr_w_fc, 1024, 4096)
    emit_bf16_copy(b, cx, W["L1_attn_w_proj"], b.scr_w_proj, 4096, 1024)


def emit_l1a_tile(b, cx, first, W):
    T, S = cx.T, cx.S
    if first:
        b.memset("dve", cx.STM, 0.0)
        for ft in range(18):
            b.memset("dve", cx.CONV[:, ft, 0:3], 0.0)
    for s in range(S):
        ln_stats_apply(b, cx, cx.LN[:, s, :], cx.X[:, s, :], cx.km[:, 0:1])
    for s in range(S):
        ln_stats_apply(b, cx, cx.SCR, cx.LN[:, s, :], cx.km[:, 3:4], do_mean=False)
        pairs = [(cx.SCR[:, f * 128:(f + 1) * 128], cx.FM1[:, f, s * 128:(s + 1) * 128])
                 for f in range(8)]
        tp128(b, cx, pairs)

    def evac_win(mt, pap):
        if mt < 16:
            b.copy("dve", cx.FMW[:, mt, :], pap)
        elif mt < 34:
            b.copy("dve", cx.CONV[:, mt - 16, 3:3 + T], pap)
        else:
            b.copy("dve", cx.dts[:, 0:T], pap[0:32, :])
    stream_mm(b, cx, b.scr_w_in, 1024, 4384, lambda k: cx.FM1[:, k, :], evac_win)
    for s in range(S):
        pairs = [(cx.FMW[:, m, s * 128:(s + 1) * 128],
                  cx.TMB[:, s * 2048 + m * 128:s * 2048 + (m + 1) * 128])
                 for m in range(16)]
        tp128(b, cx, pairs)
    for ft in range(18):
        acc = cx.SCR[:, 0:T]
        b.ts("pool", acc, cx.CONV[:, ft, 0:T], cx.convw[:, ft, 0:1], None, ALU.mult)
        for k in range(1, 4):
            b.stt("dve", acc, cx.CONV[:, ft, k:k + T], cx.convw[:, ft, k:k + 1],
                  acc, ALU.mult, ALU.add)
        b.act(cx.XC[:, ft, :], acc, AF.Silu, bias=cx.convb[:, ft:ft + 1])
        b.copy("dve", cx.CONV[:, ft, 0:3], cx.CONV[:, ft, T:T + 3])
    d0, d1, d2, d3 = (cx.dts[:, i * T:(i + 1) * T] for i in range(4))
    b.ts("dve", d1, d0, cx.dtb, None, ALU.add)
    b.act(d2, d1, AF.Abs)
    b.act(d3, d2, AF.Exp, scale=-1.0)
    b.act(d2, d3, AF.Ln, bias=cx.km[0:32, 2:3])
    b.ts("dve", d3, d1, 0.0, None, ALU.max)
    b.tt("dve", d0, d3, d2, ALU.add)
    b.ts("dve", cx.a_s, d0, cx.nega, None, ALU.mult)
    for c in range(S):
        cc = slice(c * 128, (c + 1) * 128)
        b.scan(cx.acum[:, cc], cx.a_s[:, cc], cx.zeros32.broadcast_to([32, 128]),
               0.0, ALU.add, ALU.add)
        bankT = cx.pool.take()
        b.tp(bankT[:, 0:32], cx.acum[:, cc], cx.iden[0:32, 0:32])
        b.tp(bankT[:, 32:64], d0[:, cc], cx.iden[0:32, 0:32])
        b.copy("dve", cx.acumT, bankT[:, 0:32])
        b.copy("dve", cx.dtT, bankT[:, 32:64])
        bankB = cx.pool.take().bitcast(BF16)
        b.tp(bankB[:, 0:128], cx.XC[:, 16, cc], cx.iden_bf)
        b.copy("dve", cx.Btok, bankB[:, 0:128])
        bankA = cx.pool.take()
        b.mm(bankA[:, 0:128], cx.XC[:, 16, cc], cx.XC[:, 17, cc])
        b.copy("dve", cx.As, bankA[:, 0:128])
        for h in range(32):
            hb = (h % 2) * 64
            ft = h // 2
            st_h = cx.STM[:, h * 64:(h + 1) * 64]
            bankR = cx.pool.take()
            b.tp(bankR[0:1, 0:128], cx.acumT[:, h:h + 1], cx.iden)
            b.copy("dve", cx.rowbuf, bankR[0:1, 0:128])
            bankRB = cx.pool.take()
            b.mm(bankRB[:, 0:128], cx.ones_row, cx.rowbuf)
            b.ts("dve", cx.D1s, bankRB[:, 0:128], cx.acumT[:, h:h + 1], None, ALU.subtract)
            b.tt("pool", cx.Ms, cx.D1s, cx.trilneg, ALU.add)
            b.act(cx.QS[:, 0:128], cx.Ms, AF.Exp)
            b.tt("pool", cx.AMs, cx.As, cx.QS[:, 0:128], ALU.mult)
            b.act(cx.ccol[:, 0:1], cx.D1s[:, 127:128], AF.Exp)
            b.stt("pool", cx.ccol[:, 1:2], cx.D1s[:, 127:128], 1.0, cx.acumT[:, h:h + 1],
                  ALU.mult, ALU.add)
            b.act(cx.ccol[:, 2:3], cx.ccol[:, 1:2], AF.Exp)
            b.act(cx.ccol[:, 3:4], cx.acumT[:, h:h + 1], AF.Exp)
            bank2 = cx.pool.take().bitcast(BF16)
            b.tp(bank2[:, 0:64], cx.XC[hb:hb + 64, ft, cc], cx.iden2_bf[hb:hb + 64, :])
            b.copy("dve", cx.xht, bank2[:, 0:64])
            b.ts("pool", cx.utok, cx.xht, cx.dtT[:, h:h + 1], None, ALU.mult)
            b.ts("pool", cx.KDs, cx.utok, cx.ccol[:, 0:1], None, ALU.mult)
            bankY = cx.pool.take()
            b.mm(bankY[:, 0:64], cx.AMs, cx.utok)
            b.mm(bankY[:, 64:128], cx.XC[:, 17, cc], st_h)
            ytm = cx.TMA[:, c, h * 64:(h + 1) * 64]
            b.act(cx.ytmp, bankY[:, 64:128], AF.Identity, scale=cx.ccol[:, 3:4])
            b.tt("dve", ytm, bankY[:, 0:64], cx.ytmp, ALU.add)
            b.stt("pool", ytm, cx.xht, cx.DSKC[:, h:h + 1], ytm, ALU.mult, ALU.add)
            bankS = cx.pool.take()
            b.mm(bankS[:, 0:64], cx.Btok, cx.KDs)
            b.stt("dve", st_h, st_h, cx.ccol[:, 2:3], bankS[:, 0:64], ALU.mult, ALU.add)
    for s in range(S):
        ys = cx.TMA[:, s, 0:2048]
        for hh in range(2):
            zsl = cx.TMB[:, s * 2048 + hh * 1024: s * 2048 + (hh + 1) * 1024]
            b.act(cx.SCR, zsl, AF.Silu)
            b.tt("dve", ys[:, hh * 1024:(hh + 1) * 1024], ys[:, hh * 1024:(hh + 1) * 1024],
                 cx.SCR, ALU.mult)
        b.act(cx.SCR, ys[:, 0:1024], AF.Square, accum_out=cx.cols[:, 6:7])
        b.act(cx.SCR, ys[:, 1024:2048], AF.Square, accum_out=cx.cols[:, 7:8])
        b.tt("dve", cx.cols[:, 6:7], cx.cols[:, 6:7], cx.cols[:, 7:8], ALU.add)
        b.ts("dve", cx.cols[:, 6:7], cx.cols[:, 6:7], 1.0 / 2048, None, ALU.mult)
        b.act(cx.cols[:, 6:7], cx.cols[:, 6:7], AF.Ln, bias=cx.km[:, 0:1])
        b.act(cx.cols[:, 6:7], cx.cols[:, 6:7], AF.Exp, scale=cx.km[:, 1:2])
        b.ts("dve", ys, ys, cx.cols[:, 6:7], None, ALU.mult)
        pairs = [(ys[:, f * 128:(f + 1) * 128], cx.FMW[:, f, s * 128:(s + 1) * 128])
                 for f in range(16)]
        tp128(b, cx, pairs)

    def evac_wo(mt, pap):
        b.copy("dve", cx.FM1[:, mt, :], pap)
    stream_mm(b, cx, b.scr_woutF, 2048, 1024, lambda k: cx.FMW[:, k, :], evac_wo)
    pairs = []
    for s in range(S):
        for f in range(8):
            pairs.append((cx.FM1[:, f, s * 128:(s + 1) * 128],
                          cx.LN[:, s, f * 128:(f + 1) * 128]))
    tp128(b, cx, pairs, add=True)
    for s in range(S):
        ln_stats_apply(b, cx, cx.SCR, cx.LN[:, s, :], cx.km[:, 3:4], do_mean=False)
        pairs = [(cx.SCR[:, f * 128:(f + 1) * 128], cx.FM1[:, f, s * 128:(s + 1) * 128])
                 for f in range(8)]
        tp128(b, cx, pairs)

    def evac_fc(mt, pap):
        b.act(cx.MID[:, mt, :], pap, AF.Relu)
        b.tt("dve", cx.MID[:, mt, :], cx.MID[:, mt, :], cx.MID[:, mt, :], ALU.mult)
    stream_mm(b, cx, b.scr_w_fc, 1024, 4096, lambda k: cx.FM1[:, k, :], evac_fc)

    def evac_pj(mt, pap):
        b.copy("dve", cx.FMW[:, mt, :], pap)
    stream_mm(b, cx, b.scr_w_proj, 4096, 1024, lambda k: cx.MID[:, k, :], evac_pj)
    pairs = []
    for s in range(S):
        for f in range(8):
            pairs.append((cx.FMW[:, f, s * 128:(s + 1) * 128],
                          cx.LN[:, s, f * 128:(f + 1) * 128]))
    tp128(b, cx, pairs, add=True)
    for s in range(S):
        b.tt("dve", cx.X[:, s, :], cx.X[:, s, :], cx.LN[:, s, :], ALU.add)




def emit_l1f_setup(b, cx, W):
    for nm, dst in [("L1_ffn_ln_w", cx.LNWc), ("L1_ffn_ln_b", cx.LNBc)]:
        b.dma("sp", cx.ROWS[0:1, 0:1024], W[nm].unsqueeze(0), "ld_small", batch=True)
        tp_cols(b, cx, [(cx.ROWS[0:1, f * 128:(f + 1) * 128], dst[:, f:f + 1])
                        for f in range(8)])
    b.dma("pool", cx.RW, W["L1_ffn_router_w"].rearrange("(kp p) e -> p kp e", p=128),
          "ld_cast", batch=True)
    emit_bf16_copy(b, cx, W["L1_ffn_gate_w"].rearrange("e i o -> (e i) o"), b.scr_gate, 8192, 2048)
    emit_bf16_copy(b, cx, W["L1_ffn_up_w"].rearrange("e i o -> (e i) o"), b.scr_up, 8192, 2048)
    emit_bf16_copy(b, cx, W["L1_ffn_down_w"].rearrange("e i o -> (e i) o"), b.scr_down, 16384, 1024)
    emit_bf16_copy(b, cx, W["L1_ffn_sh_gate"], b.scr_shg, 1024, 2048)
    emit_bf16_copy(b, cx, W["L1_ffn_sh_up"], b.scr_shu, 1024, 2048)
    emit_bf16_copy(b, cx, W["L1_ffn_sh_down"], b.scr_shd, 2048, 1024)


def emit_l1f_tile(b, cx, W):
    T, S = cx.T, cx.S
    for s in range(S):
        ln_stats_apply(b, cx, cx.LN[:, s, :], cx.X[:, s, :], cx.km[:, 0:1])
    for s in range(S):
        ln_stats_apply(b, cx, cx.SCR, cx.LN[:, s, :], cx.km[:, 0:1])
        pairs = [(cx.SCR[:, f * 128:(f + 1) * 128], cx.FMW[:, f, s * 128:(s + 1) * 128])
                 for f in range(8)]
        tp128(b, cx, pairs)
        for f in range(8):
            b.ts("dve", cx.FMW[:, f, s * 128:(s + 1) * 128],
                 cx.FMW[:, f, s * 128:(s + 1) * 128],
                 cx.LNWc[:, f:f + 1], cx.LNBc[:, f:f + 1], ALU.mult, ALU.add)
        bank = cx.pool.take()
        for k in range(8):
            b.mm(bank[:, 0:8], cx.FMW[:, k, s * 128:(s + 1) * 128], cx.RW[:, k, :],
                 start=(k == 0), stop=(k == 7))
        b.copy("dve", cx.LG[:, s, :], bank[:, 0:8])
        b.rmax(cx.cols[:, 12:13], cx.LG[:, s, :])
        b.act(cx.cols[:, 13 + s:14 + s], cx.cols[:, 12:13], AF.Sigmoid)
        b.ts("dve", cx.WCOL[:, s, :], cx.LG[:, s, :], cx.cols[:, 12:13], None, ALU.is_ge)
    i = 0
    while i < S:
        grp = list(range(i, min(i + 4, S)))
        bank = cx.pool.take()
        for j, s in enumerate(grp):
            b.tp(bank[0:1, j * 128:j * 128 + 128], cx.cols[:, 13 + s:14 + s], cx.iden)
        for j, s in enumerate(grp):
            b.copy("dve", cx.wrow[0:1, s * 128:(s + 1) * 128], bank[0:1, j * 128:j * 128 + 128])
        i += 4
    bank = cx.pool.take()
    b.mm(bank[:, 0:T], cx.ones_row, cx.wrow[0:1, 0:T])
    b.copy("dve", cx.SCB, bank[:, 0:T])
    for f in range(8):
        b.tt("dve", cx.FMW[:, 8 + f, :], cx.FMW[:, f, :], cx.SCB, ALU.mult)

    def evac_g(mt, pap):
        b.act(cx.MID[:, mt, :], pap, AF.Silu)

    def mk_evac_u(masked):
        def evac_u(mt, pap):
            b.tt("dve", cx.MID[:, mt, :], cx.MID[:, mt, :], pap, ALU.mult)
            if masked:
                b.tt("pool", cx.MID[:, mt, :], cx.MID[:, mt, :], cx.WBs, ALU.mult)
        return evac_u

    def mk_evac_d(first):
        def evac_d(mt, pap):
            if first:
                b.copy("dve", cx.FM1[:, mt, :], pap)
            else:
                b.tt("dve", cx.FM1[:, mt, :], pap, cx.FM1[:, mt, :], ALU.add)
        return evac_d

    for e in range(8):
        pairs = [(cx.WCOL[:, s, e:e + 1], None) for s in range(S)]
        i = 0
        while i < len(pairs):
            grp = pairs[i:i + 4]
            bank = cx.pool.take()
            for j, (cap, _) in enumerate(grp):
                b.tp(bank[0:1, j * 128:j * 128 + 128], cap, cx.iden)
            for j in range(len(grp)):
                b.copy("dve", cx.wrow[0:1, (i + j) * 128:(i + j + 1) * 128],
                       bank[0:1, j * 128:j * 128 + 128])
            i += 4
        bank = cx.pool.take()
        b.mm(bank[:, 0:T], cx.ones_row, cx.wrow[0:1, 0:T])
        b.copy("dve", cx.WBs, bank[:, 0:T])
        stream_mm(b, cx, b.scr_gate[e * 1024:(e + 1) * 1024, :], 1024, 2048,
                  lambda k: cx.FMW[:, 8 + k, :], evac_g)
        stream_mm(b, cx, b.scr_up[e * 1024:(e + 1) * 1024, :], 1024, 2048,
                  lambda k: cx.FMW[:, 8 + k, :], mk_evac_u(True))
        stream_mm(b, cx, b.scr_down[e * 2048:(e + 1) * 2048, :], 2048, 1024,
                  lambda k: cx.MID[:, k, :], mk_evac_d(e == 0))
    stream_mm(b, cx, b.scr_shg, 1024, 2048, lambda k: cx.FMW[:, k, :], evac_g)
    stream_mm(b, cx, b.scr_shu, 1024, 2048, lambda k: cx.FMW[:, k, :], mk_evac_u(False))
    stream_mm(b, cx, b.scr_shd, 2048, 1024, lambda k: cx.MID[:, k, :], mk_evac_d(False))
    pairs = []
    for s in range(S):
        for f in range(8):
            pairs.append((cx.FM1[:, f, s * 128:(s + 1) * 128],
                          cx.LN[:, s, f * 128:(f + 1) * 128]))
    tp128(b, cx, pairs, add=True)
    for s in range(S):
        b.tt("dve", cx.X[:, s, :], cx.X[:, s, :], cx.LN[:, s, :], ALU.add)


def build_program(blocks, seqs, Lc, T):
    b = Prog("mk")
    cx = make_ctx(b, T, blocks)
    consts = {k: b.dram_in(k, list(v)) for k, v in CONST_SPECS.items()}
    xin = b.dram_in("xin", [seqs, Lc, 1024])
    xout = b.dram_out("xout", [seqs, Lc, 1024])
    W = {}
    if "l0a" in blocks:
        W["L0_attn_Wqkv_w"] = b.dram_in("L0_attn_Wqkv_w", [3072, 1024])
        W["L0_attn_Wqkv_b"] = b.dram_in("L0_attn_Wqkv_b", [3072])
        W["L0_attn_out_w"] = b.dram_in("L0_attn_out_w", [1024, 1024])
        W["L0_attn_out_b"] = b.dram_in("L0_attn_out_b", [1024])
        W["L0_attn_gamma_raw"] = b.dram_in("L0_attn_gamma_raw", [16])
        b.scr_wqkvT = b.dram_tmp("scr_wqkvT", [1024, 3072], BF16)
        b.scr_outT = b.dram_tmp("scr_outT", [1024, 1024], BF16)
    if "l0f" in blocks:
        W["L0_ffn_w1"] = b.dram_in("L0_ffn_w1", [4096, 1024])
        W["L0_ffn_w2"] = b.dram_in("L0_ffn_w2", [1024, 4096])
        b.scr_w1qT = b.dram_tmp("scr_w1qT", [1024, 4096], BF16)
        b.scr_w2qT = b.dram_tmp("scr_w2qT", [4096, 1024], BF16)
    if "l1a" in blocks:
        for nm, sh in [("L1_attn_w_in", [1024, 4384]), ("L1_attn_conv_w", [2304, 4]),
                       ("L1_attn_conv_b", [2304]), ("L1_attn_dt_bias", [32]),
                       ("L1_attn_a_log", [32]), ("L1_attn_d_skip", [32]),
                       ("L1_attn_norm_w", [2048]), ("L1_attn_w_out", [2048, 1024]),
                       ("L1_attn_w_fc", [1024, 4096]), ("L1_attn_w_proj", [4096, 1024])]:
            W[nm] = b.dram_in(nm, sh)
        b.scr_woutF = b.dram_tmp("scr_woutF", [2048, 1024], BF16)
        b.scr_w_in = b.dram_tmp("scr_w_in", [1024, 4384], BF16)
        b.scr_w_fc = b.dram_tmp("scr_w_fc", [1024, 4096], BF16)
        b.scr_w_proj = b.dram_tmp("scr_w_proj", [4096, 1024], BF16)
    if "l1f" in blocks:
        for nm, sh in [("L1_ffn_ln_w", [1024]), ("L1_ffn_ln_b", [1024]),
                       ("L1_ffn_router_w", [1024, 8]), ("L1_ffn_gate_w", [8, 1024, 2048]),
                       ("L1_ffn_up_w", [8, 1024, 2048]), ("L1_ffn_down_w", [8, 2048, 1024]),
                       ("L1_ffn_sh_gate", [1024, 2048]), ("L1_ffn_sh_up", [1024, 2048]),
                       ("L1_ffn_sh_down", [2048, 1024])]:
            W[nm] = b.dram_in(nm, sh)
        b.scr_gate = b.dram_tmp("scr_gate", [8 * 1024, 2048], BF16)
        b.scr_up = b.dram_tmp("scr_up", [8 * 1024, 2048], BF16)
        b.scr_down = b.dram_tmp("scr_down", [8 * 2048, 1024], BF16)
        b.scr_shg = b.dram_tmp("scr_shg", [1024, 2048], BF16)
        b.scr_shu = b.dram_tmp("scr_shu", [1024, 2048], BF16)
        b.scr_shd = b.dram_tmp("scr_shd", [2048, 1024], BF16)

    emit_consts(b, cx, consts)
    if "l0a" in blocks:
        emit_l0a_setup(b, cx, W)
    if "l0f" in blocks:
        emit_l0f_setup(b, cx, W)
    if "l1a" in blocks:
        emit_l1a_setup(b, cx, W)
    if "l1f" in blocks:
        emit_l1f_setup(b, cx, W)

    ntiles = Lc // T
    for sq in range(seqs):
        for t in range(ntiles):
            b.dma("act", cx.X,
                  xin[sq, t * T:(t + 1) * T, :].rearrange("(s p) d -> p s d", p=128),
                  "ld_x")
            if "l0a" in blocks:
                emit_l0a_tile(b, cx, first=(t == 0), W=W)
            if "l0f" in blocks:
                emit_l0f_tile(b, cx, W)
            if "l1a" in blocks:
                emit_l1a_tile(b, cx, first=(t == 0), W=W)
            if "l1f" in blocks:
                emit_l1f_tile(b, cx, W)
            b.dma("act", xout[sq, t * T:(t + 1) * T, :].rearrange("(s p) d -> p s d", p=128),
                  cx.X, "out_x")
    return b



def _run_launch(blocks, x, winputs, seqs_per_core=2, T=512):
    Lc = x.shape[1]
    b = build_program(blocks, seqs_per_core, Lc, T)
    nc = b.finalize()
    consts = host_consts()
    ncores = x.shape[0] // seqs_per_core
    in_maps = []
    for c in range(ncores):
        m = dict(consts)
        m.update(winputs)
        m["xin"] = np.ascontiguousarray(x[c * seqs_per_core:(c + 1) * seqs_per_core])
        in_maps.append(m)
    res = run_bass_kernel_spmd(nc, in_maps, core_ids=list(range(ncores)))
    return np.concatenate([r["xout"] for r in res.results], axis=0)


def kernel(**inputs):
    x = np.ascontiguousarray(np.asarray(inputs["L0_attn_x"], np.float32))
    w = {k: np.ascontiguousarray(np.asarray(v, np.float32))
         for k, v in inputs.items() if k != "L0_attn_x"}
    return _run_launch(["l0a", "l0f", "l1a", "l1f"], x, w, seqs_per_core=2, T=512)


# version 03 (final)
# speedup vs baseline: 1.1823x; 1.1823x over previous
import numpy as np
import concourse.bass as bass
import concourse.mybir as mybir
from concourse.bass_utils import run_bass_kernel_spmd

F32 = mybir.dt.float32
BF16 = mybir.dt.bfloat16
I32 = mybir.dt.int32
AF = mybir.ActivationFunctionType
ALU = mybir.AluOpType
AX = mybir.AxisListType

ENGINES = ("pe", "dve", "act", "pool", "sp")
SECTION = {"pe": "tensor", "dve": "vector", "act": "scalar", "pool": "gpsimd", "sp": "sync"}


class Prog:
    def __init__(self, name="prog"):
        self.nc = bass.Bass("TRN2", target_bir_lowering=False, debug=False)
        self.name = name
        self.ops = {e: [] for e in ENGINES}
        self.engine_count = {e: 0 for e in ENGINES}
        self.sems = {}
        self.res_count = {}
        self.res_queue = {}
        self.mem = {}
        self.whole = set()
        self.readonly = set()
        self.free_elems = {}
        self.engine_vc = {e: {} for e in ENGINES}
        self.op_vc = {}
        for e in ENGINES:
            self.sems["e_" + e] = self.nc.alloc_semaphore("e_" + e)

    def sbuf(self, name, shape, dtype=F32):
        h = self.nc.alloc_sbuf_tensor(name, list(shape), dtype)
        self.free_elems[name] = int(np.prod(shape[1:]))
        return h.ap()

    def psum(self, name, shape, dtype=F32):
        h = self.nc.alloc_psum_tensor(name, list(shape), dtype)
        self.free_elems[name] = int(np.prod(shape[1:]))
        self.whole.add(name)
        return h.ap()

    def dram_in(self, name, shape, dtype=F32):
        ap = self.nc.dram_tensor(name, list(shape), dtype, kind="ExternalInput").ap()
        self.free_elems[name] = int(np.prod(shape))
        self.readonly.add(name)
        return ap

    def dram_out(self, name, shape, dtype=F32):
        ap = self.nc.dram_tensor(name, list(shape), dtype, kind="ExternalOutput").ap()
        self.free_elems[name] = int(np.prod(shape))
        return ap

    def dram_tmp(self, name, shape, dtype=F32):
        ap = self.nc.dram_tensor(name, list(shape), dtype, kind="Internal").ap()
        self.free_elems[name] = int(np.prod(shape))
        return ap

    def resource(self, name, queue):
        if name not in self.res_count:
            self.sems[name] = self.nc.alloc_semaphore(name)
            self.res_count[name] = 0
            self.res_queue[name] = queue
        return name

    def _region(self, ap):
        t = ap.tensor
        name = t.name
        fe = self.free_elems.get(name)
        if name in self.whole or fe is None:
            return (name, 0, 1 << 20, 0, 1 << 40)
        kind = type(t).__name__
        if "SBTensor" in kind or "PSum" in kind:
            try:
                p0 = ap.base_partition()
                psz = ap.partition_size()
                off = ap.offset - p0 * fe
                pdim = None
                for i, (st, ct) in enumerate(ap.ap):
                    if st == fe and ct == psz:
                        pdim = i
                        break
                span = 0
                for i, (st, ct) in enumerate(ap.ap):
                    if i == pdim:
                        continue
                    span += abs(st) * (ct - 1)
                if pdim is None and psz != 1:
                    return (name, 0, 1 << 20, 0, 1 << 40)
                return (name, p0, p0 + psz, off, off + span + 1)
            except Exception:
                return (name, 0, 1 << 20, 0, 1 << 40)
        off = ap.offset
        span = 0
        for (st, ct) in ap.ap:
            span += abs(st) * (ct - 1)
        return (name, 0, 1 << 20, off, off + span + 1)

    @staticmethod
    def _overlap(a, b):
        return a[1] < b[2] and b[1] < a[2] and a[3] < b[4] and b[3] < a[4]

    @staticmethod
    def _covers(a, b):
        return a[1] <= b[1] and a[2] >= b[2] and a[3] <= b[3] and a[4] >= b[4]

    def _track(self, reads, writes):
        deps = []
        rregions = []
        wregions = [self._region(a) for a in writes]
        for a in reads:
            if a.tensor.name in self.readonly:
                continue
            if a.tensor.name in self.whole:
                wregions.append(self._region(a))
            else:
                rregions.append(self._region(a))
        for r in rregions:
            for (reg, iswrite, tag) in self.mem.get(r[0], ()):
                if iswrite and self._overlap(reg, r):
                    deps.append(tag)
        for w in wregions:
            for (reg, iswrite, tag) in self.mem.get(w[0], ()):
                if self._overlap(reg, w):
                    deps.append(tag)
        deps = [(s, v) if s.startswith("e_") else (s, self.res_count[s])
                for (s, v) in deps]
        return rregions, wregions, deps

    def _commit(self, rregions, wregions, tag):
        for r in rregions:
            self.mem.setdefault(r[0], []).append((r, False, tag))
        for w in wregions:
            recs = self.mem.setdefault(w[0], [])
            recs[:] = [rec for rec in recs if not self._covers(w, rec[0])]
            recs.append((w, True, tag))

    def _reduce_waits(self, engine, deps):
        own = "e_" + engine
        vc = self.engine_vc.setdefault(engine, {})
        waits = []
        for (s, v) in sorted(set(deps), key=lambda d: (d[0], -d[1])):
            if vc.get(s, 0) >= v:
                continue
            waits.append((s, v))
            vc[s] = v
            pvc = self.op_vc.get((s, v))
            if pvc:
                for (s2, v2) in pvc.items():
                    if s2 == own:
                        continue
                    if vc.get(s2, 0) < v2:
                        vc[s2] = v2
        return waits

    def op(self, engine, emit, reads=(), writes=(), extra_deps=()):
        rr, wr, deps = self._track(list(reads), list(writes))
        deps = deps + list(extra_deps)
        if engine == "pe":
            deps = [d for d in deps if d[0] != "e_pe"]
        waits = self._reduce_waits(engine, deps)
        self.engine_count[engine] += 1
        tag = ("e_" + engine, self.engine_count[engine])
        snap = dict(self.engine_vc[engine])
        snap[tag[0]] = tag[1]
        self.op_vc[tag] = snap
        self.ops[engine].append(dict(emit=emit, waits=waits, sem="e_" + engine, inc=1))
        self._commit(rr, wr, tag)
        return tag

    def dma(self, queue, out, in_, res, slow=False, batch=False, extra_deps=()):
        self.resource(res, queue)
        rr, wr, deps = self._track([in_], [out])
        deps = deps + list(extra_deps)
        waits = self._reduce_waits(queue, deps)
        if self.res_count[res] > 0:
            w = (res, self.res_count[res])
            if w not in waits:
                waits = waits + [w]
            vc = self.engine_vc[queue]
            if vc.get(res, 0) < w[1]:
                vc[res] = w[1]
        self.res_count[res] += 16
        tag = (res, self.res_count[res])
        snap = dict(self.engine_vc[queue])
        snap[res] = self.res_count[res]
        self.op_vc[tag] = snap

        def emit(eng, out=out, in_=in_, slow=slow):
            if slow:
                return eng.dma_start(out=out, in_=in_, allow_slow_non_contiguous=True)
            return eng.dma_start(out=out, in_=in_)
        self.ops[queue].append(dict(emit=emit, waits=waits, sem=res, inc=16))
        self._commit(rr, wr, tag)
        return tag

    def finalize(self):
        nc = self.nc
        stats = {e: len(self.ops[e]) for e in ENGINES}
        stats["waits"] = 0
        with nc.Block() as block:
            for engine in ENGINES:
                ops = self.ops[engine]
                final_res = [(r, c) for r, c in self.res_count.items()
                             if c > 0 and self.res_queue[r] == engine and r.startswith("out")]
                if not ops and not final_res:
                    continue
                section = getattr(block, SECTION[engine])

                def body(eng, ops=ops, engine=engine, final_res=final_res):
                    for o in ops:
                        for (s, v) in o["waits"]:
                            eng.wait_ge(self.sems[s], v)
                            stats["waits"] += 1
                        inst = o["emit"](eng)
                        inst.then_inc(self.sems[o["sem"]], o["inc"])
                    for (r, c) in final_res:
                        eng.wait_ge(self.sems[r], c)
                section(body)
        self.stats = stats
        return nc

    def tt(self, engine, out, a, b, alu):
        return self.op(engine, lambda e: e.tensor_tensor(out, a, b, alu),
                       reads=[a, b], writes=[out])

    def ts(self, engine, out, in0, s1, s2, op0, op1=ALU.bypass):
        reads = [in0] + [x for x in (s1, s2) if hasattr(x, "tensor")]
        return self.op(engine, lambda e: e.tensor_scalar(out, in0, s1, s2, op0, op1),
                       reads=reads, writes=[out])

    def stt(self, engine, out, in0, scalar, in1, op0, op1):
        engine = "dve"
        reads = [in0, in1] + ([scalar] if hasattr(scalar, "tensor") else [])
        return self.op(engine, lambda e: e.scalar_tensor_tensor(out, in0, scalar, in1, op0, op1),
                       reads=reads, writes=[out])

    def copy(self, engine, out, in_):
        if engine == "act":
            return self.op("act", lambda e: e.copy(out, in_), reads=[in_], writes=[out])
        return self.op(engine, lambda e: e.tensor_copy(out, in_), reads=[in_], writes=[out])

    def memset(self, engine, ap, val):
        return self.op(engine, lambda e: e.memset(ap, val), reads=[], writes=[ap])

    def act(self, out, in_, func, bias=0.0, scale=1.0, accum_out=None):
        reads = [in_] + [x for x in (bias, scale) if hasattr(x, "tensor")]
        writes = [out] + ([accum_out] if accum_out is not None else [])
        return self.op("act", lambda e: e.activation(out, in_, func, bias=bias, scale=scale,
                                                     accum_out=accum_out),
                       reads=reads, writes=writes)

    def rsum(self, out, in_, engine="dve"):
        return self.op(engine, lambda e: e.reduce_sum(out, in_, AX.X),
                       reads=[in_], writes=[out])

    def rmax(self, out, in_, absval=False, engine="dve"):
        return self.op(engine, lambda e: e.reduce_max(out, in_, AX.X, apply_absolute_value=absval),
                       reads=[in_], writes=[out])

    def scan(self, out, d0, d1, init, op0, op1, engine="dve"):
        return self.op(engine, lambda e: e.tensor_tensor_scan(out, d0, d1, init, op0, op1),
                       reads=[d0, d1], writes=[out])

    def mm(self, out, lhsT, rhs, start=True, stop=True):
        return self.op("pe", lambda e: e.matmul(out, lhsT, rhs, start=start, stop=stop),
                       reads=[lhsT, rhs], writes=[out])

    def tp(self, out, in_, ident):
        return self.op("pe", lambda e: e.transpose(out, in_, ident),
                       reads=[in_, ident], writes=[out])


class PsumPool:
    def __init__(self, b):
        self.banks = [b.psum("pbank%d" % i, [128, 512], F32) for i in range(8)]
        self.i = 0
        self.reserved = 0

    def take(self):
        for _ in range(8):
            j = self.i % 8
            self.i += 1
            if j >= self.reserved:
                return self.banks[j]
        raise RuntimeError("no free banks")

    def fixed(self, j):
        return self.banks[j]


def host_consts():
    ii = np.arange(128)
    c = {}
    c["c_iden"] = np.eye(128, dtype=np.float32)
    c["c_iden2"] = np.vstack([np.eye(64, dtype=np.float32)] * 2)
    c["c_ones_row"] = np.ones((1, 128), dtype=np.float32)
    c["c_ones_col"] = np.ones((128, 1), dtype=np.float32)
    c["c_imj"] = (ii[None, :] - ii[:, None]).astype(np.float32)
    c["c_trilneg"] = np.where(ii[:, None] <= ii[None, :], 0.0, -3e38).astype(np.float32)
    c["c_ip1"] = np.tile((ii + 1).astype(np.float32)[None, :], (128, 1))
    c["c_rev"] = (127.0 - ii).astype(np.float32)[:, None]
    return c


CONST_SPECS = {k: v.shape for k, v in host_consts().items()}


class Ctx:
    pass


def make_ctx(b, T, blocks):
    cx = Ctx()
    cx.T = T
    S = T // 128
    cx.S = S
    cx.pool = PsumPool(b)
    cx.iden = b.sbuf("iden", [128, 128])
    cx.iden2 = b.sbuf("iden2", [128, 64])
    cx.iden_bf = b.sbuf("iden_bf", [128, 128], BF16)
    cx.iden2_bf = b.sbuf("iden2_bf", [128, 64], BF16)
    cx.ones_row = b.sbuf("ones_row", [1, 128])
    cx.ones_col = b.sbuf("ones_col", [128, 1])
    cx.imj = b.sbuf("imj", [128, 128])
    cx.trilneg = b.sbuf("trilneg", [128, 128])
    cx.ip1 = b.sbuf("ip1", [128, 128])
    cx.rev = b.sbuf("rev", [128, 1])
    cx.km = b.sbuf("km", [128, 8])
    cx.X = b.sbuf("X", [128, S, 1024])
    cx.LN = b.sbuf("LN", [128, S, 1024])
    cx.SCR = b.sbuf("SCR", [128, 1024])
    cx.SCR2 = b.sbuf("SCR2", [128, 1024])
    cx.TMA = b.sbuf("TMA", [128, S, 2048], BF16)
    cx.TMB = b.sbuf("TMB", [128, max(S, 2) * 2048], BF16)
    cx.FM1 = b.sbuf("FM1", [128, 8, T], BF16)
    cx.FMW = b.sbuf("FMW", [128, 16, T], BF16)
    cx.MID = b.sbuf("MID", [128, 32, T], BF16)
    cx.cols = b.sbuf("cols", [128, 20])
    cx.QS = b.sbuf("QS", [128, 1024])
    cx.ring = [b.sbuf("wslot%d" % i, [128, 512], BF16) for i in range(5)]
    cx.ring_i = 0
    cx.D1s4 = [b.sbuf("D1s4_%d" % i, [128, 512]) for i in range(2)]
    cx.AMs4 = [b.sbuf("AMs4_%d" % i, [128, 512], BF16) for i in range(2)]
    cx.Ktok4 = [b.sbuf("Ktok4_%d" % i, [128, 256], BF16) for i in range(2)]
    cx.Vtok4 = [b.sbuf("Vtok4_%d" % i, [128, 256], BF16) for i in range(2)]
    cx.KDs4 = [b.sbuf("KDs4_%d" % i, [128, 256], BF16) for i in range(2)]
    cx.QGs4 = [b.sbuf("QGs4_%d" % i, [128, 256], BF16) for i in range(2)]
    cx.utok4 = [b.sbuf("utok4_%d" % i, [128, 256], BF16) for i in range(2)]
    cx.xht4 = [b.sbuf("xht4_%d" % i, [128, 256], BF16) for i in range(2)]
    cx.YS4 = [b.sbuf("YS4_%d" % i, [128, 256]) for i in range(2)]
    cx.EC4 = [b.sbuf("EC4_%d" % i, [128, 16]) for i in range(2)]
    cx.As = [b.sbuf("As_%d" % i, [128, 128]) for i in range(2)]
    cx.Btok = [b.sbuf("Btok_%d" % i, [128, 128], BF16) for i in range(2)]
    cx.rowbuf = [b.sbuf("rowbuf_%d" % i, [1, 512]) for i in range(2)]
    cx.ccol = b.sbuf("ccol", [128, 4])
    if "l0a" in blocks:
        cx.STR = b.sbuf("STR", [128, 8 * 128], BF16)
        cx.DMASK = b.sbuf("DMASK", [128, 16 * 128], BF16)
        cx.G1 = b.sbuf("G1", [128, 8 * 128])
        cx.DECAYK = b.sbuf("DECAYK", [128, 16])
        cx.GC = b.sbuf("GC", [128, 16])
        cx.LNG = b.sbuf("LNG", [128, 16])
        cx.qkvb = b.sbuf("qkvb", [128, 24])
        cx.outb = b.sbuf("outb", [128, 8])
    if "l1a" in blocks:
        cx.CONV = b.sbuf("CONV", [128, 18, T + 4], BF16)
        cx.STM = b.sbuf("STM", [128, 32 * 64], BF16)
        cx.convw = b.sbuf("convw", [128, 18, 4])
        cx.convb = b.sbuf("convb", [128, 18])
        cx.dtb = b.sbuf("dtb", [32, 1])
        cx.nega = b.sbuf("nega", [32, 1])
        cx.DSKC = b.sbuf("DSKC", [128, 32])
        cx.dts = b.sbuf("dts", [32, 4 * T])
        cx.acum = cx.dts[:, 2 * T:3 * T]
        cx.a_s = cx.dts[:, 1 * T:2 * T]
        cx.acumT = [b.sbuf("acumT_%d" % i, [128, 32]) for i in range(2)]
        cx.dtT = [b.sbuf("dtT_%d" % i, [128, 32]) for i in range(2)]
        cx.zeros32 = b.sbuf("zeros32", [32, 1])
        cx.ROWS = b.sbuf("ROWS", [1, 1024])
        cx.NWC = b.sbuf("NWC", [128, 16])
    if "l1f" in blocks:
        if not hasattr(cx, "ROWS"):
            cx.ROWS = b.sbuf("ROWS", [1, 1024])
        cx.LNWc = b.sbuf("LNWc", [128, 8])
        cx.LNBc = b.sbuf("LNBc", [128, 8])
        cx.SCB = b.sbuf("SCB", [128, T], BF16)
        cx.WBs = cx.SCB
        cx.RW = b.sbuf("RW", [128, 8, 8], BF16)
        cx.LG = b.sbuf("LG", [128, S, 8])
        cx.WCOL = b.sbuf("WCOL", [128, S, 8])
        cx.wrow = cx.ROWS[0:1, 0:T] if hasattr(cx, "ROWS") else b.sbuf("wrow", [1, T])
    if "l1a" in blocks:
        cx.XC = cx.MID[:, 0:18, :]
    return cx


def ring_next(cx):
    s = cx.ring[cx.ring_i % 5]
    cx.ring_i += 1
    return s, "wr%d" % ((cx.ring_i - 1) % 5)



def emit_consts(b, cx, consts_aps):
    for name, ap in consts_aps.items():
        dst = getattr(cx, name[2:])
        b.dma("sp", dst, ap, "ld_const", batch=True)
    vals = [1e-5, -0.5, 1.0, 1.1920929e-07, 128.0, 0.5, float(np.log(1.0 / 8.0)), 1e-5]
    for i, v in enumerate(vals):
        b.memset("dve", cx.km[:, i:i + 1], v)
    b.copy("dve", cx.iden_bf, cx.iden)
    b.copy("dve", cx.iden2_bf, cx.iden2)


def ln_stats_apply(b, cx, dst, src, eps_col, do_mean=True):
    c = cx.cols
    mean, msq, var, rstd = c[:, 0:1], c[:, 1:2], c[:, 2:3], c[:, 3:4]
    n = src.shape[-1]
    if do_mean:
        b.rsum(mean, src)
        b.ts("dve", mean, mean, 1.0 / n, None, ALU.mult)
    b.act(cx.SCR2[:, 0:n], src, AF.Square, accum_out=msq)
    b.ts("dve", msq, msq, 1.0 / n, None, ALU.mult)
    if do_mean:
        b.tt("dve", var, mean, mean, ALU.mult)
        b.tt("dve", var, msq, var, ALU.subtract)
    else:
        b.copy("dve", var, msq)
    b.act(rstd, var, AF.Ln, bias=eps_col)
    b.act(rstd, rstd, AF.Exp, scale=cx.km[:, 1:2])
    if do_mean:
        b.ts("dve", dst, src, mean, rstd, ALU.subtract, ALU.mult)
    else:
        b.ts("dve", dst, src, rstd, None, ALU.mult)


def tp128(b, cx, pairs, add=False, evac="dve"):
    i = 0
    while i < len(pairs):
        grp = pairs[i:i + 4]
        bank = cx.pool.take()
        views = []
        for j, (s, d) in enumerate(grp):
            if s.dtype == BF16:
                pa = bank.bitcast(BF16)[:, j * 256:j * 256 + 128]
                b.tp(pa, s, cx.iden_bf)
            else:
                pa = bank[:, j * 128:(j + 1) * 128]
                b.tp(pa, s, cx.iden)
            views.append(pa)
        for j, (s, d) in enumerate(grp):
            if add:
                b.tt(evac, d, views[j], d, ALU.add)
            else:
                b.copy(evac if j % 2 == 0 else "act", d, views[j])
        i += 4


def tm_to_fm(b, cx, fm, tm, nf):
    pairs = []
    for s in range(cx.S):
        for f in range(nf):
            pairs.append((tm[:, s, f * 128:(f + 1) * 128],
                          fm[:, f, s * 128:(s + 1) * 128]))
    tp128(b, cx, pairs)


def stream_mm(b, cx, wsrc, K, M, rhs_fn, evac_fn):
    T = cx.T
    nk = (K + 127) // 128
    mtiles = []
    m0 = 0
    while m0 < M:
        msz = min(128, M - m0)
        mtiles.append((m0, msz))
        m0 += msz
    gi = 0
    while gi < len(mtiles):
        grp = mtiles[gi:gi + 4]
        paps = [cx.pool.take()[0:msz, 0:T] for (m0, msz) in grp]
        for k in range(nk):
            ksz = min(128, K - k * 128)
            wcols = sum(msz for (_, msz) in grp)
            slot, res = ring_next(cx)
            b.dma("sp", slot[0:ksz, 0:wcols],
                  wsrc[k * 128:k * 128 + ksz, grp[0][0]:grp[0][0] + wcols], res)
            co = 0
            for j, (m0, msz) in enumerate(grp):
                b.mm(paps[j], slot[0:ksz, co:co + msz], rhs_fn(k)[0:ksz, :],
                     start=(k == 0), stop=(k == nk - 1))
                co += msz
        for j, (m0, msz) in enumerate(grp):
            evac_fn(gi + j, paps[j])
        gi += 4



L0A_SETUP = 99

def emit_l0a_setup(b, cx, W):
    if globals()["L0A_SETUP"] < 1:
        return
    b.dma("sp", cx.qkvb, W["L0_attn_Wqkv_b"].rearrange("(t p) -> p t", p=128), "ld_small", slow=True, batch=True)
    b.dma("sp", cx.outb, W["L0_attn_out_b"].rearrange("(t p) -> p t", p=128), "ld_small", slow=True, batch=True)
    grow = cx.rowbuf[0]
    b.dma("sp", grow[0:1, 0:16], W["L0_attn_gamma_raw"].unsqueeze(0), "ld_small", batch=True)
    t = cx.ccol
    r0 = cx.SCR[0:1, 0:16]
    r1 = cx.SCR[0:1, 16:32]
    r2 = cx.SCR[0:1, 32:48]
    b.act(r0, grow[0:1, 0:16], AF.Abs)
    b.act(r0, r0, AF.Exp, scale=-1.0)
    b.act(r0, r0, AF.Ln, bias=1.0)
    b.ts("dve", r1, grow[0:1, 0:16], -1.0, 0.0, ALU.mult, ALU.max)
    b.tt("dve", r2, r1, r0, ALU.add)
    b.ts("dve", r2, r2, -1.0, None, ALU.mult)
    bank = cx.pool.take()
    b.mm(bank[:, 0:16], cx.ones_row, r2)
    b.copy("dve", cx.LNG, bank[:, 0:16])
    for h in range(16):
        lc = cx.LNG[:, h:h + 1]
        ph = (h // 4) * 4 + (h % 2) * 2 + ((h // 2) % 2)
        dm = cx.DMASK[:, ph * 128:(ph + 1) * 128]
        b.ts("dve", dm, cx.imj, lc, cx.km[:, 6:7], ALU.mult, ALU.add)
        b.tt("pool", dm, dm, cx.trilneg, ALU.add)
        b.act(dm, dm, AF.Exp)
        hb = (h % 2) * 64
        g1 = cx.G1[hb:hb + 64, (h // 2) * 128:(h // 2 + 1) * 128]
        b.ts("dve", g1, cx.ip1[hb:hb + 64, :], lc[hb:hb + 64, :], None, ALU.mult)
        b.act(g1, g1, AF.Exp)
        b.ts("dve", cx.DECAYK[:, ph:ph + 1], cx.rev, lc, cx.km[:, 6:7], ALU.mult, ALU.add)
        b.act(cx.DECAYK[:, ph:ph + 1], cx.DECAYK[:, ph:ph + 1], AF.Exp)
        b.act(cx.GC[hb:hb + 64, h // 2:h // 2 + 1], lc[hb:hb + 64, :], AF.Exp,
              scale=cx.km[hb:hb + 64, 4:5])
    if globals()["L0A_SETUP"] < 2:
        return
    emit_transpose_prep(b, cx, W["L0_attn_Wqkv_w"], b.scr_wqkvT, 3072, 1024)
    emit_transpose_prep(b, cx, W["L0_attn_out_w"], b.scr_outT, 1024, 1024)


def emit_transpose_prep(b, cx, wsrc, wdst, O, I, quant=None):
    CC = min(I, 1024)
    for r in range(O // 128):
        for cc in range(I // CC):
            b.dma("sp", cx.SCR[:, 0:CC], wsrc[r * 128:(r + 1) * 128, cc * CC:(cc + 1) * CC],
                  "ld_prep")
            src = cx.SCR[:, 0:CC]
            if quant is not None:
                scol, thrcol = quant
                b.act(cx.SCR2, src, AF.Sign)
                b.act(cx.QS, src, AF.Abs)
                b.ts("dve", cx.QS, cx.QS, thrcol, None, ALU.is_ge)
                b.tt("pool", cx.SCR2, cx.SCR2, cx.QS, ALU.mult)
                b.ts("dve", cx.SCR2, cx.SCR2, scol, None, ALU.mult)
                src = cx.SCR2[:, 0:CC]
            pairs = []
            dst = cx.TMB[:, 4096:4096 + CC]
            for cb in range(CC // 128):
                pairs.append((src[:, cb * 128:(cb + 1) * 128], dst[:, cb * 128:(cb + 1) * 128]))
            tp128(b, cx, pairs)
            b.dma("sp", wdst.rearrange("(c p) o -> p c o", p=128)[:, cc * (CC // 128):(cc + 1) * (CC // 128), r * 128:(r + 1) * 128],
                  dst.rearrange("p (c m) -> p c m", m=128), "st_prep", batch=True)


def emit_bf16_copy(b, cx, wsrc, wdst, R, C):
    for r in range((R + 127) // 128):
        rs = min(128, R - r * 128)
        b.dma("pool", cx.TMB[0:rs, 0:C], wsrc[r * 128:r * 128 + rs, :], "ld_cast")
        b.dma("sp", wdst[r * 128:r * 128 + rs, :], cx.TMB[0:rs, 0:C], "st_prep", batch=True)


def emit_wmean_scale(b, cx, wsrc, O, I, scol, thrcol):
    CC = min(I, 1024)
    bank0 = cx.pool.take()
    bank1 = cx.pool.take()
    nblk = (O // 128) * (I // CC)
    i = 0
    for r in range(O // 128):
        for cc in range(I // CC):
            b.dma("sp", cx.SCR[:, 0:CC], wsrc[r * 128:(r + 1) * 128, cc * CC:(cc + 1) * CC],
                  "ld_prep")
            b.act(cx.SCR2, cx.SCR[:, 0:CC], AF.Abs)
            b.mm(bank0[0:1, 0:512], cx.ones_col, cx.SCR2[:, 0:512],
                 start=(i == 0), stop=(i == nblk - 1))
            b.mm(bank1[0:1, 0:512], cx.ones_col, cx.SCR2[:, 512:1024],
                 start=(i == 0), stop=(i == nblk - 1))
            i += 1
    rows = cx.SCR[0:1, 0:1024]
    b.copy("dve", rows[0:1, 0:512], bank0[0:1, 0:512])
    b.copy("dve", rows[0:1, 512:1024], bank1[0:1, 0:512])
    b.rsum(cx.QS[0:1, 0:1], rows)
    srow = cx.QS[0:1, 2:4]
    b.ts("dve", srow[0:1, 0:1], cx.QS[0:1, 0:1], 1.0 / (O * I), 1e-5,
         ALU.mult, ALU.max)
    b.ts("dve", srow[0:1, 1:2], srow[0:1, 0:1], 0.5, None, ALU.mult)
    bank = cx.pool.take()
    b.mm(bank[:, 0:2], cx.ones_row, srow)
    b.copy("dve", scol, bank[:, 0:1])
    b.copy("dve", thrcol, bank[:, 1:2])


def quant_act_tm(b, cx, strips, full_row_ap):
    c = cx.cols
    amax, rec = c[:, 4:5], c[:, 5:6]
    b.rmax(amax, full_row_ap, absval=True)
    b.ts("dve", amax, amax, 1e-5, None, ALU.max)
    b.op("dve", lambda e: e.reciprocal(rec, amax), reads=[amax], writes=[rec])
    for st in strips:
        n = st.shape[-1]
        v = cx.SCR[:, 0:n]
        b.ts("dve", v, st, rec, 127.0, ALU.mult, ALU.mult)
        i32 = cx.SCR2.bitcast(I32)[:, 0:n]
        b.copy("dve", i32, v)
        b.copy("dve", st, i32)
        f = cx.SCR2[:, 0:n]
        b.tt("dve", f, v, st, ALU.subtract)
        g1 = cx.QS[:, 0:n]
        b.ts("pool", g1, f, 0.5, None, ALU.is_ge)
        b.ts("dve", f, f, -0.5, None, ALU.is_le)
        b.tt("dve", st, st, g1, ALU.add)
        b.tt("pool", st, st, f, ALU.subtract)
        b.ts("dve", st, st, -128.0, 127.0, ALU.max, ALU.min)
        b.ts("dve", st, st, amax, 1.0 / 127.0, ALU.mult, ALU.mult)


L0A_DEPTH = 99
L0A_CH = 99

def emit_l0a_tile(b, cx, first, W):
    T, S = cx.T, cx.S
    if first:
        b.memset("dve", cx.STR, 0.0)
    if globals()["L0A_DEPTH"] < 1:
        return
    for s in range(S):
        ln_stats_apply(b, cx, cx.LN[:, s, :], cx.X[:, s, :], cx.km[:, 0:1])
    if globals()["L0A_DEPTH"] < 2:
        return
    tm_to_fm(b, cx, cx.FM1, cx.LN, 8)
    def evac_qkv(mt, pap):
        b.act(cx.MID[:, mt, :], pap, AF.Identity, bias=cx.qkvb[:, mt:mt + 1])
    stream_mm(b, cx, b.scr_wqkvT, 1024, 3072, lambda k: cx.FM1[:, k, :], evac_qkv)
    if globals()["L0A_DEPTH"] < 3:
        return
    for c in range(S):
        cc = slice(c * 128, (c + 1) * 128)
        for g in range(4):
            pp = g % 2
            h0 = g * 4
            t0 = h0 // 2
            qv = cx.MID[:, t0:t0 + 2, cc]
            kv = cx.MID[:, 8 + t0:8 + t0 + 2, cc]
            vv = cx.MID[:, 16 + t0:16 + t0 + 2, cc]
            AM = cx.AMs4[pp]
            bankA1 = cx.pool.take()
            bankA2 = cx.pool.take()
            for i in range(2):
                b.mm(bankA1[:, i * 128:(i + 1) * 128], kv[0:64, i, :], qv[0:64, i, :])
                b.mm(bankA2[:, i * 128:(i + 1) * 128], kv[64:128, i, :], qv[64:128, i, :])
            b.tt("dve", AM[:, 0:256], bankA1[:, 0:256],
                 cx.DMASK[:, g * 512:g * 512 + 256], ALU.mult)
            b.tt("dve", AM[:, 256:512], bankA2[:, 0:256],
                 cx.DMASK[:, g * 512 + 256:(g + 1) * 512], ALU.mult)
            if globals()["L0A_CH"] < 2:
                continue
            bankT1 = cx.pool.take().bitcast(BF16)
            bankT2 = cx.pool.take().bitcast(BF16)
            for i in range(2):
                b.tp(bankT1[:, i * 64:(i + 1) * 64], kv[0:64, i, :], cx.iden2_bf[0:64, :])
                b.tp(bankT1[:, 256 + i * 64:256 + (i + 1) * 64], vv[0:64, i, :], cx.iden2_bf[0:64, :])
                b.tp(bankT2[:, i * 64:(i + 1) * 64], kv[64:128, i, :], cx.iden2_bf[64:128, :])
                b.tp(bankT2[:, 256 + i * 64:256 + (i + 1) * 64], vv[64:128, i, :], cx.iden2_bf[64:128, :])
            Kt, Vt = cx.Ktok4[pp], cx.Vtok4[pp]
            b.copy("dve", Kt[:, 0:128], bankT1[:, 0:128])
            b.copy("act", Vt[:, 0:128], bankT1[:, 256:384])
            b.copy("dve", Kt[:, 128:256], bankT2[:, 0:128])
            b.copy("act", Vt[:, 128:256], bankT2[:, 256:384])
            KD = cx.KDs4[pp]
            b.tt("pool", KD.rearrange("p (h d) -> p h d", d=64),
                 Kt.rearrange("p (h d) -> p h d", d=64),
                 cx.DECAYK[:, 4 * g:4 * g + 4].unsqueeze(2).broadcast_to([128, 4, 64]),
                 ALU.mult)
            QG = cx.QGs4[pp]
            b.tt("pool", QG.rearrange("p (t n) -> p t n", n=128), qv,
                 cx.G1[:, t0 * 128:(t0 + 2) * 128].rearrange("p (t n) -> p t n", n=128),
                 ALU.mult)
            if globals()["L0A_CH"] < 3:
                continue
            bankY = cx.pool.take()
            for j in range(4):
                pj = (j % 2) * 2 + j // 2
                b.mm(bankY[:, pj * 64:(pj + 1) * 64], AM[:, pj * 128:(pj + 1) * 128],
                     Vt[:, pj * 64:(pj + 1) * 64])
            for i in range(2):
                b.mm(bankY[:, 256 + i * 128:256 + (i + 1) * 128],
                     QG.rearrange("p (t n) -> p t n", n=128)[:, i, :],
                     cx.STR[:, (t0 + i) * 128:(t0 + i + 1) * 128])
            ytm = cx.TMA[:, c, h0 * 64:(h0 + 4) * 64].rearrange(
                "p (a par d) -> p par a d", par=2, d=64)
            y1v = bankY[:, 0:256].rearrange("p (par a d) -> p par a d", a=2, d=64)
            y2v = bankY[:, 256:512].rearrange("p (a par d) -> p par a d", par=2, d=64)
            b.copy("dve", ytm, y1v)
            b.tt("dve", ytm, y2v, ytm, ALU.add)
            if globals()["L0A_CH"] < 4:
                continue
            bankS = cx.pool.take()
            b.memset("dve", bankS[:, 0:256], 0.0)
            for j in range(4):
                pj = (j % 2) * 2 + j // 2
                hb = (j % 2) * 64
                b.mm(bankS[hb:hb + 64, (j // 2) * 128 + hb:(j // 2) * 128 + hb + 64],
                     KD[:, pj * 64:(pj + 1) * 64], Vt[:, pj * 64:(pj + 1) * 64])
            STRv = cx.STR[:, t0 * 128:(t0 + 2) * 128]
            YS = cx.YS4[pp]
            b.tt("dve", YS.rearrange("p (t n) -> p t n", n=128),
                 STRv.rearrange("p (t n) -> p t n", n=128),
                 cx.GC[:, t0:t0 + 2].unsqueeze(2).broadcast_to([128, 2, 128]), ALU.mult)
            b.tt("dve", STRv, YS, bankS[:, 0:256], ALU.add)
    if globals()["L0A_DEPTH"] < 4:
        return
    pairs = []
    for s in range(S):
        for f in range(8):
            pairs.append((cx.TMA[:, s, f * 128:(f + 1) * 128],
                          cx.FM1[:, f, s * 128:(s + 1) * 128]))
    tp128(b, cx, pairs)
    def evac_out(mt, pap):
        b.act(cx.FMW[:, mt, :], pap, AF.Identity, bias=cx.outb[:, mt:mt + 1])
    stream_mm(b, cx, b.scr_outT, 1024, 1024, lambda k: cx.FM1[:, k, :], evac_out)
    pairs = []
    for s in range(S):
        for f in range(8):
            pairs.append((cx.FMW[:, f, s * 128:(s + 1) * 128],
                          cx.X[:, s, f * 128:(f + 1) * 128]))
    tp128(b, cx, pairs, add=True)





def emit_l0f_setup(b, cx, W):
    emit_wmean_scale(b, cx, W["L0_ffn_w1"], 4096, 1024, cx.cols[:, 8:9], cx.cols[:, 9:10])
    emit_transpose_prep(b, cx, W["L0_ffn_w1"], b.scr_w1qT, 4096, 1024,
                        quant=(cx.cols[:, 8:9], cx.cols[:, 9:10]))
    emit_wmean_scale(b, cx, W["L0_ffn_w2"], 1024, 4096, cx.cols[:, 10:11], cx.cols[:, 11:12])
    emit_transpose_prep(b, cx, W["L0_ffn_w2"], b.scr_w2qT, 1024, 4096,
                        quant=(cx.cols[:, 10:11], cx.cols[:, 11:12]))


def emit_l0f_tile(b, cx, W):
    T, S = cx.T, cx.S
    for s in range(S):
        ln_stats_apply(b, cx, cx.LN[:, s, :], cx.X[:, s, :], cx.km[:, 0:1])
        quant_act_tm(b, cx, [cx.LN[:, s, :]], cx.LN[:, s, :])
    tm_to_fm(b, cx, cx.FM1, cx.LN, 8)

    def evac_h(mt, pap):
        b.act(cx.MID[:, mt, :], pap, AF.Gelu)
    stream_mm(b, cx, b.scr_w1qT, 1024, 4096, lambda k: cx.FM1[:, k, :], evac_h)
    for s in range(S):
        pairs = [(cx.MID[:, m, s * 128:(s + 1) * 128], cx.TMB[:, m * 128:(m + 1) * 128])
                 for m in range(32)]
        tp128(b, cx, pairs)
        quant_act_tm(b, cx, [cx.TMB[:, j * 1024:(j + 1) * 1024] for j in range(4)],
                     cx.TMB[:, 0:4096])
        pairs = [(cx.TMB[:, m * 128:(m + 1) * 128], cx.MID[:, m, s * 128:(s + 1) * 128])
                 for m in range(32)]
        tp128(b, cx, pairs)

    def evac_o(mt, pap):
        b.copy("dve", cx.FMW[:, mt, :], pap)
    stream_mm(b, cx, b.scr_w2qT, 4096, 1024, lambda k: cx.MID[:, k, :], evac_o)
    pairs = []
    for s in range(S):
        for f in range(8):
            pairs.append((cx.FMW[:, f, s * 128:(s + 1) * 128],
                          cx.X[:, s, f * 128:(f + 1) * 128]))
    tp128(b, cx, pairs, add=True)




def tp_cols(b, cx, pairs):
    i = 0
    while i < len(pairs):
        grp = pairs[i:i + 4]
        bank = cx.pool.take()
        for j, (r, d) in enumerate(grp):
            b.tp(bank[0:d.shape[0], j:j + 1], r, cx.ones_row[0:1, 0:1])
        for j, (r, d) in enumerate(grp):
            b.copy("dve", d, bank[0:d.shape[0], j:j + 1])
        i += 4


def emit_l1a_setup(b, cx, W):
    cwflat = W["L1_attn_conv_w"].rearrange("a b -> (a b)").unsqueeze(0)
    for t in range(18):
        b.dma("sp", cx.ROWS[0:1, 0:512], cwflat[0:1, t * 512:(t + 1) * 512], "ld_small", batch=True)
        RV = cx.ROWS[0:1, 0:512].rearrange("r (p k) -> r p k", k=4)
        pairs = [(RV[0:1, :, k], cx.convw[:, t, k:k + 1]) for k in range(4)]
        tp_cols(b, cx, pairs)
    for ch in range(3):
        n = min(1024, 2304 - ch * 1024)
        b.dma("sp", cx.ROWS[0:1, 0:n], W["L1_attn_conv_b"][ch * 1024:ch * 1024 + n].unsqueeze(0),
              "ld_small", batch=True)
        tp_cols(b, cx, [(cx.ROWS[0:1, i * 128:(i + 1) * 128], cx.convb[:, ch * 8 + i:ch * 8 + i + 1])
                        for i in range(n // 128)])
    b.dma("sp", cx.ROWS[0:1, 0:32], W["L1_attn_dt_bias"].unsqueeze(0), "ld_small", batch=True)
    b.dma("sp", cx.ROWS[0:1, 64:96], W["L1_attn_a_log"].unsqueeze(0), "ld_small", batch=True)
    b.dma("sp", cx.ROWS[0:1, 128:160], W["L1_attn_d_skip"].unsqueeze(0), "ld_small", batch=True)
    tp_cols(b, cx, [(cx.ROWS[0:1, 0:32], cx.dtb),
                    (cx.ROWS[0:1, 64:96], cx.nega)])
    b.act(cx.nega, cx.nega, AF.Exp)
    b.ts("dve", cx.nega, cx.nega, -1.0, None, ALU.mult)
    bank = cx.pool.take()
    b.mm(bank[:, 0:32], cx.ones_row, cx.ROWS[0:1, 128:160])
    b.copy("dve", cx.DSKC, bank[:, 0:32])
    b.memset("dve", cx.zeros32, 0.0)
    for ch in range(2):
        b.dma("sp", cx.ROWS[0:1, 0:1024], W["L1_attn_norm_w"][ch * 1024:(ch + 1) * 1024].unsqueeze(0),
              "ld_small", batch=True)
        tp_cols(b, cx, [(cx.ROWS[0:1, i * 128:(i + 1) * 128], cx.NWC[:, ch * 8 + i:ch * 8 + i + 1])
                        for i in range(8)])
    for r in range(16):
        b.dma("sp", cx.SCR, W["L1_attn_w_out"][r * 128:(r + 1) * 128, :], "ld_prep")
        b.ts("dve", cx.TMB[:, 0:1024], cx.SCR, cx.NWC[:, r:r + 1], None, ALU.mult)
        b.dma("sp", b.scr_woutF[r * 128:(r + 1) * 128, :], cx.TMB[:, 0:1024], "st_prep", batch=True)
    emit_bf16_copy(b, cx, W["L1_attn_w_in"], b.scr_w_in, 1024, 4384)
    emit_bf16_copy(b, cx, W["L1_attn_w_fc"], b.scr_w_fc, 1024, 4096)
    emit_bf16_copy(b, cx, W["L1_attn_w_proj"], b.scr_w_proj, 4096, 1024)


def emit_l1a_tile(b, cx, first, W):
    T, S = cx.T, cx.S
    if first:
        b.memset("dve", cx.STM, 0.0)
        for ft in range(18):
            b.memset("dve", cx.CONV[:, ft, 0:3], 0.0)
    for s in range(S):
        ln_stats_apply(b, cx, cx.LN[:, s, :], cx.X[:, s, :], cx.km[:, 0:1])
    for s in range(S):
        ln_stats_apply(b, cx, cx.SCR, cx.LN[:, s, :], cx.km[:, 3:4], do_mean=False)
        pairs = [(cx.SCR[:, f * 128:(f + 1) * 128], cx.FM1[:, f, s * 128:(s + 1) * 128])
                 for f in range(8)]
        tp128(b, cx, pairs)

    def evac_win(mt, pap):
        if mt < 16:
            b.copy("dve", cx.FMW[:, mt, :], pap)
        elif mt < 34:
            b.copy("dve", cx.CONV[:, mt - 16, 3:3 + T], pap)
        else:
            b.copy("dve", cx.dts[:, 0:T], pap[0:32, :])
    stream_mm(b, cx, b.scr_w_in, 1024, 4384, lambda k: cx.FM1[:, k, :], evac_win)
    for s in range(S):
        pairs = [(cx.FMW[:, m, s * 128:(s + 1) * 128],
                  cx.TMB[:, s * 2048 + m * 128:s * 2048 + (m + 1) * 128])
                 for m in range(16)]
        tp128(b, cx, pairs)
    for ft in range(18):
        acc = cx.SCR[:, 0:T]
        b.ts("pool", acc, cx.CONV[:, ft, 0:T], cx.convw[:, ft, 0:1], None, ALU.mult)
        for k in range(1, 4):
            b.stt("dve", acc, cx.CONV[:, ft, k:k + T], cx.convw[:, ft, k:k + 1],
                  acc, ALU.mult, ALU.add)
        b.act(cx.XC[:, ft, :], acc, AF.Silu, bias=cx.convb[:, ft:ft + 1])
        b.copy("dve", cx.CONV[:, ft, 0:3], cx.CONV[:, ft, T:T + 3])
    d0, d1, d2, d3 = (cx.dts[:, i * T:(i + 1) * T] for i in range(4))
    b.ts("dve", d1, d0, cx.dtb, None, ALU.add)
    b.act(d2, d1, AF.Abs)
    b.act(d3, d2, AF.Exp, scale=-1.0)
    b.act(d2, d3, AF.Ln, bias=cx.km[0:32, 2:3])
    b.ts("dve", d3, d1, 0.0, None, ALU.max)
    b.tt("dve", d0, d3, d2, ALU.add)
    b.ts("dve", cx.a_s, d0, cx.nega, None, ALU.mult)
    for c in range(S):
        cc = slice(c * 128, (c + 1) * 128)
        cp = c % 2
        b.scan(cx.acum[:, cc], cx.a_s[:, cc], cx.zeros32.broadcast_to([32, 128]),
               0.0, ALU.add, ALU.add)
        bankT = cx.pool.take()
        b.tp(bankT[:, 0:32], cx.acum[:, cc], cx.iden[0:32, 0:32])
        b.tp(bankT[:, 32:64], d0[:, cc], cx.iden[0:32, 0:32])
        acumT, dtT = cx.acumT[cp], cx.dtT[cp]
        b.copy("dve", acumT, bankT[:, 0:32])
        b.copy("act", dtT, bankT[:, 32:64])
        bankB = cx.pool.take().bitcast(BF16)
        b.tp(bankB[:, 0:128], cx.XC[:, 16, cc], cx.iden_bf)
        Btok = cx.Btok[cp]
        b.copy("dve", Btok, bankB[:, 0:128])
        bankA = cx.pool.take()
        b.mm(bankA[:, 0:128], cx.XC[:, 16, cc], cx.XC[:, 17, cc])
        As = cx.As[cp]
        b.copy("act", As, bankA[:, 0:128])
        for g in range(8):
            pp = g % 2
            h0 = g * 4
            t0 = h0 // 2
            bankR = cx.pool.take()
            for j in range(4):
                b.tp(bankR[0:1, j * 128:(j + 1) * 128], acumT[:, h0 + j:h0 + j + 1], cx.iden)
            rb = cx.rowbuf[pp]
            b.copy("dve", rb, bankR[0:1, 0:512])
            bankRB = cx.pool.take()
            b.mm(bankRB[:, 0:512], cx.ones_row, rb)
            D1 = cx.D1s4[pp]
            b.tt("dve", D1.rearrange("p (h n) -> p h n", n=128),
                 bankRB[:, 0:512].rearrange("p (h n) -> p h n", n=128),
                 acumT[:, h0:h0 + 4].unsqueeze(2).broadcast_to([128, 4, 128]),
                 ALU.subtract)
            Ms = cx.QS[:, 0:512]
            b.tt("pool", Ms.rearrange("p (h n) -> p h n", n=128),
                 D1.rearrange("p (h n) -> p h n", n=128),
                 cx.trilneg.unsqueeze(1).broadcast_to([128, 4, 128]), ALU.add)
            b.act(Ms, Ms, AF.Exp)
            AM = cx.AMs4[pp]
            b.tt("pool", AM.rearrange("p (h n) -> p h n", n=128),
                 As.unsqueeze(1).broadcast_to([128, 4, 128]),
                 Ms.rearrange("p (h n) -> p h n", n=128), ALU.mult)
            EC = cx.EC4[pp]
            dcv = D1.rearrange("p (h n) -> p h n", n=128)[:, :, 127]
            b.act(EC[:, 0:4], dcv, AF.Exp)
            b.act(EC[:, 4:8], acumT[:, h0:h0 + 4], AF.Exp)
            b.tt("dve", EC[:, 12:16], dcv, acumT[:, h0:h0 + 4], ALU.add)
            b.act(EC[:, 8:12], EC[:, 12:16], AF.Exp)
            bankX1 = cx.pool.take().bitcast(BF16)
            bankX2 = cx.pool.take().bitcast(BF16)
            for i in range(2):
                b.tp(bankX1[:, i * 64:(i + 1) * 64], cx.XC[0:64, t0 + i, cc],
                     cx.iden2_bf[0:64, :])
                b.tp(bankX2[:, i * 64:(i + 1) * 64], cx.XC[64:128, t0 + i, cc],
                     cx.iden2_bf[64:128, :])
            xht = cx.xht4[pp]
            b.copy("dve", xht[:, 0:128], bankX1[:, 0:128])
            b.copy("act", xht[:, 128:256], bankX2[:, 0:128])
            xht4d = xht.rearrange("p (par a d) -> p par a d", par=2, d=64)
            ut = cx.utok4[pp]
            ut4d = ut.rearrange("p (par a d) -> p par a d", par=2, d=64)

            def parv(tbl):
                return tbl.rearrange("p (a par) -> p par a", par=2).unsqueeze(3)\
                          .broadcast_to([128, 2, 2, 64])
            b.tt("pool", ut4d, xht4d, parv(dtT[:, h0:h0 + 4]), ALU.mult)
            du = cx.KDs4[pp]
            du4d = du.rearrange("p (par a d) -> p par a d", par=2, d=64)
            b.tt("pool", du4d, ut4d, parv(EC[:, 0:4]), ALU.mult)
            bankY = cx.pool.take()
            for j in range(4):
                pj = (j % 2) * 2 + j // 2
                b.mm(bankY[:, j * 128:j * 128 + 64], AM[:, j * 128:(j + 1) * 128],
                     ut[:, pj * 64:(pj + 1) * 64])
                b.mm(bankY[:, j * 128 + 64:(j + 1) * 128], cx.XC[:, 17, cc],
                     cx.STM[:, (h0 + j) * 64:(h0 + j + 1) * 64])
            yv = bankY[:, 0:512].rearrange("p (h t d) -> p t h d", t=2, d=64)
            YS = cx.YS4[pp]
            ysv = YS.rearrange("p (h d) -> p h d", d=64)
            b.tt("dve", ysv, yv[:, 1],
                 EC[:, 4:8].unsqueeze(2).broadcast_to([128, 4, 64]), ALU.mult)
            ytm = cx.TMA[:, c, h0 * 64:(h0 + 4) * 64].rearrange("p (h d) -> p h d", d=64)
            b.tt("dve", ytm, yv[:, 0], ysv, ALU.add)
            ys4d = YS.rearrange("p (a par d) -> p par a d", par=2, d=64)
            b.tt("pool", ys4d, xht4d,
                 cx.DSKC[:, h0:h0 + 4].rearrange("p (a par) -> p par a", par=2)
                 .unsqueeze(3).broadcast_to([128, 2, 2, 64]), ALU.mult)
            b.tt("dve", ytm, ytm, ysv, ALU.add)
            bankS = cx.pool.take()
            for j in range(4):
                pj = (j % 2) * 2 + j // 2
                b.mm(bankS[:, j * 64:(j + 1) * 64], Btok, du[:, pj * 64:(pj + 1) * 64])
            STMv = cx.STM[:, h0 * 64:(h0 + 4) * 64]
            b.tt("dve", ysv, STMv.rearrange("p (h d) -> p h d", d=64),
                 EC[:, 8:12].unsqueeze(2).broadcast_to([128, 4, 64]), ALU.mult)
            b.tt("dve", STMv, YS, bankS[:, 0:256], ALU.add)
    for s in range(S):
        ys = cx.TMA[:, s, 0:2048]
        for hh in range(2):
            zsl = cx.TMB[:, s * 2048 + hh * 1024: s * 2048 + (hh + 1) * 1024]
            b.act(cx.SCR, zsl, AF.Silu)
            b.tt("dve", ys[:, hh * 1024:(hh + 1) * 1024], ys[:, hh * 1024:(hh + 1) * 1024],
                 cx.SCR, ALU.mult)
        b.act(cx.SCR, ys[:, 0:1024], AF.Square, accum_out=cx.cols[:, 6:7])
        b.act(cx.SCR, ys[:, 1024:2048], AF.Square, accum_out=cx.cols[:, 7:8])
        b.tt("dve", cx.cols[:, 6:7], cx.cols[:, 6:7], cx.cols[:, 7:8], ALU.add)
        b.ts("dve", cx.cols[:, 6:7], cx.cols[:, 6:7], 1.0 / 2048, None, ALU.mult)
        b.act(cx.cols[:, 6:7], cx.cols[:, 6:7], AF.Ln, bias=cx.km[:, 0:1])
        b.act(cx.cols[:, 6:7], cx.cols[:, 6:7], AF.Exp, scale=cx.km[:, 1:2])
        b.ts("dve", ys, ys, cx.cols[:, 6:7], None, ALU.mult)
        pairs = [(ys[:, f * 128:(f + 1) * 128], cx.FMW[:, f, s * 128:(s + 1) * 128])
                 for f in range(16)]
        tp128(b, cx, pairs)

    def evac_wo(mt, pap):
        b.copy("dve", cx.FM1[:, mt, :], pap)
    stream_mm(b, cx, b.scr_woutF, 2048, 1024, lambda k: cx.FMW[:, k, :], evac_wo)
    pairs = []
    for s in range(S):
        for f in range(8):
            pairs.append((cx.FM1[:, f, s * 128:(s + 1) * 128],
                          cx.LN[:, s, f * 128:(f + 1) * 128]))
    tp128(b, cx, pairs, add=True)
    for s in range(S):
        ln_stats_apply(b, cx, cx.SCR, cx.LN[:, s, :], cx.km[:, 3:4], do_mean=False)
        pairs = [(cx.SCR[:, f * 128:(f + 1) * 128], cx.FM1[:, f, s * 128:(s + 1) * 128])
                 for f in range(8)]
        tp128(b, cx, pairs)

    def evac_fc(mt, pap):
        b.act(cx.MID[:, mt, :], pap, AF.Relu)
        b.tt("dve", cx.MID[:, mt, :], cx.MID[:, mt, :], cx.MID[:, mt, :], ALU.mult)
    stream_mm(b, cx, b.scr_w_fc, 1024, 4096, lambda k: cx.FM1[:, k, :], evac_fc)

    def evac_pj(mt, pap):
        b.copy("dve", cx.FMW[:, mt, :], pap)
    stream_mm(b, cx, b.scr_w_proj, 4096, 1024, lambda k: cx.MID[:, k, :], evac_pj)
    pairs = []
    for s in range(S):
        for f in range(8):
            pairs.append((cx.FMW[:, f, s * 128:(s + 1) * 128],
                          cx.LN[:, s, f * 128:(f + 1) * 128]))
    tp128(b, cx, pairs, add=True)
    for s in range(S):
        b.tt("dve", cx.X[:, s, :], cx.X[:, s, :], cx.LN[:, s, :], ALU.add)




def emit_l1f_setup(b, cx, W):
    for nm, dst in [("L1_ffn_ln_w", cx.LNWc), ("L1_ffn_ln_b", cx.LNBc)]:
        b.dma("sp", cx.ROWS[0:1, 0:1024], W[nm].unsqueeze(0), "ld_small", batch=True)
        tp_cols(b, cx, [(cx.ROWS[0:1, f * 128:(f + 1) * 128], dst[:, f:f + 1])
                        for f in range(8)])
    b.dma("pool", cx.RW, W["L1_ffn_router_w"].rearrange("(kp p) e -> p kp e", p=128),
          "ld_cast", batch=True)
    emit_bf16_copy(b, cx, W["L1_ffn_gate_w"].rearrange("e i o -> (e i) o"), b.scr_gate, 8192, 2048)
    emit_bf16_copy(b, cx, W["L1_ffn_up_w"].rearrange("e i o -> (e i) o"), b.scr_up, 8192, 2048)
    emit_bf16_copy(b, cx, W["L1_ffn_down_w"].rearrange("e i o -> (e i) o"), b.scr_down, 16384, 1024)
    emit_bf16_copy(b, cx, W["L1_ffn_sh_gate"], b.scr_shg, 1024, 2048)
    emit_bf16_copy(b, cx, W["L1_ffn_sh_up"], b.scr_shu, 1024, 2048)
    emit_bf16_copy(b, cx, W["L1_ffn_sh_down"], b.scr_shd, 2048, 1024)


def emit_l1f_tile(b, cx, W):
    T, S = cx.T, cx.S
    for s in range(S):
        ln_stats_apply(b, cx, cx.LN[:, s, :], cx.X[:, s, :], cx.km[:, 0:1])
    for s in range(S):
        ln_stats_apply(b, cx, cx.SCR, cx.LN[:, s, :], cx.km[:, 0:1])
        pairs = [(cx.SCR[:, f * 128:(f + 1) * 128], cx.FMW[:, f, s * 128:(s + 1) * 128])
                 for f in range(8)]
        tp128(b, cx, pairs)
        for f in range(8):
            b.ts("dve", cx.FMW[:, f, s * 128:(s + 1) * 128],
                 cx.FMW[:, f, s * 128:(s + 1) * 128],
                 cx.LNWc[:, f:f + 1], cx.LNBc[:, f:f + 1], ALU.mult, ALU.add)
        bank = cx.pool.take()
        for k in range(8):
            b.mm(bank[:, 0:8], cx.FMW[:, k, s * 128:(s + 1) * 128], cx.RW[:, k, :],
                 start=(k == 0), stop=(k == 7))
        b.copy("dve", cx.LG[:, s, :], bank[:, 0:8])
        b.rmax(cx.cols[:, 12:13], cx.LG[:, s, :])
        b.act(cx.cols[:, 13 + s:14 + s], cx.cols[:, 12:13], AF.Sigmoid)
        b.ts("dve", cx.WCOL[:, s, :], cx.LG[:, s, :], cx.cols[:, 12:13], None, ALU.is_ge)
    i = 0
    while i < S:
        grp = list(range(i, min(i + 4, S)))
        bank = cx.pool.take()
        for j, s in enumerate(grp):
            b.tp(bank[0:1, j * 128:j * 128 + 128], cx.cols[:, 13 + s:14 + s], cx.iden)
        for j, s in enumerate(grp):
            b.copy("dve", cx.wrow[0:1, s * 128:(s + 1) * 128], bank[0:1, j * 128:j * 128 + 128])
        i += 4
    bank = cx.pool.take()
    b.mm(bank[:, 0:T], cx.ones_row, cx.wrow[0:1, 0:T])
    b.copy("dve", cx.SCB, bank[:, 0:T])
    for f in range(8):
        b.tt("dve", cx.FMW[:, 8 + f, :], cx.FMW[:, f, :], cx.SCB, ALU.mult)

    def evac_g(mt, pap):
        b.act(cx.MID[:, mt, :], pap, AF.Silu)

    def mk_evac_u(masked):
        def evac_u(mt, pap):
            b.tt("dve", cx.MID[:, mt, :], cx.MID[:, mt, :], pap, ALU.mult)
            if masked:
                b.tt("pool", cx.MID[:, mt, :], cx.MID[:, mt, :], cx.WBs, ALU.mult)
        return evac_u

    def mk_evac_d(first):
        def evac_d(mt, pap):
            if first:
                b.copy("dve", cx.FM1[:, mt, :], pap)
            else:
                b.tt("dve", cx.FM1[:, mt, :], pap, cx.FM1[:, mt, :], ALU.add)
        return evac_d

    for e in range(8):
        pairs = [(cx.WCOL[:, s, e:e + 1], None) for s in range(S)]
        i = 0
        while i < len(pairs):
            grp = pairs[i:i + 4]
            bank = cx.pool.take()
            for j, (cap, _) in enumerate(grp):
                b.tp(bank[0:1, j * 128:j * 128 + 128], cap, cx.iden)
            for j in range(len(grp)):
                b.copy("dve", cx.wrow[0:1, (i + j) * 128:(i + j + 1) * 128],
                       bank[0:1, j * 128:j * 128 + 128])
            i += 4
        bank = cx.pool.take()
        b.mm(bank[:, 0:T], cx.ones_row, cx.wrow[0:1, 0:T])
        b.copy("dve", cx.WBs, bank[:, 0:T])
        stream_mm(b, cx, b.scr_gate[e * 1024:(e + 1) * 1024, :], 1024, 2048,
                  lambda k: cx.FMW[:, 8 + k, :], evac_g)
        stream_mm(b, cx, b.scr_up[e * 1024:(e + 1) * 1024, :], 1024, 2048,
                  lambda k: cx.FMW[:, 8 + k, :], mk_evac_u(True))
        stream_mm(b, cx, b.scr_down[e * 2048:(e + 1) * 2048, :], 2048, 1024,
                  lambda k: cx.MID[:, k, :], mk_evac_d(e == 0))
    stream_mm(b, cx, b.scr_shg, 1024, 2048, lambda k: cx.FMW[:, k, :], evac_g)
    stream_mm(b, cx, b.scr_shu, 1024, 2048, lambda k: cx.FMW[:, k, :], mk_evac_u(False))
    stream_mm(b, cx, b.scr_shd, 2048, 1024, lambda k: cx.MID[:, k, :], mk_evac_d(False))
    pairs = []
    for s in range(S):
        for f in range(8):
            pairs.append((cx.FM1[:, f, s * 128:(s + 1) * 128],
                          cx.LN[:, s, f * 128:(f + 1) * 128]))
    tp128(b, cx, pairs, add=True)
    for s in range(S):
        b.tt("dve", cx.X[:, s, :], cx.X[:, s, :], cx.LN[:, s, :], ALU.add)


def build_program(blocks, seqs, Lc, T):
    b = Prog("mk")
    cx = make_ctx(b, T, blocks)
    consts = {k: b.dram_in(k, list(v)) for k, v in CONST_SPECS.items()}
    xin = b.dram_in("xin", [seqs, Lc, 1024])
    xout = b.dram_out("xout", [seqs, Lc, 1024])
    W = {}
    if "l0a" in blocks:
        W["L0_attn_Wqkv_w"] = b.dram_in("L0_attn_Wqkv_w", [3072, 1024])
        W["L0_attn_Wqkv_b"] = b.dram_in("L0_attn_Wqkv_b", [3072])
        W["L0_attn_out_w"] = b.dram_in("L0_attn_out_w", [1024, 1024])
        W["L0_attn_out_b"] = b.dram_in("L0_attn_out_b", [1024])
        W["L0_attn_gamma_raw"] = b.dram_in("L0_attn_gamma_raw", [16])
        b.scr_wqkvT = b.dram_tmp("scr_wqkvT", [1024, 3072], BF16)
        b.scr_outT = b.dram_tmp("scr_outT", [1024, 1024], BF16)
    if "l0f" in blocks:
        W["L0_ffn_w1"] = b.dram_in("L0_ffn_w1", [4096, 1024])
        W["L0_ffn_w2"] = b.dram_in("L0_ffn_w2", [1024, 4096])
        b.scr_w1qT = b.dram_tmp("scr_w1qT", [1024, 4096], BF16)
        b.scr_w2qT = b.dram_tmp("scr_w2qT", [4096, 1024], BF16)
    if "l1a" in blocks:
        for nm, sh in [("L1_attn_w_in", [1024, 4384]), ("L1_attn_conv_w", [2304, 4]),
                       ("L1_attn_conv_b", [2304]), ("L1_attn_dt_bias", [32]),
                       ("L1_attn_a_log", [32]), ("L1_attn_d_skip", [32]),
                       ("L1_attn_norm_w", [2048]), ("L1_attn_w_out", [2048, 1024]),
                       ("L1_attn_w_fc", [1024, 4096]), ("L1_attn_w_proj", [4096, 1024])]:
            W[nm] = b.dram_in(nm, sh)
        b.scr_woutF = b.dram_tmp("scr_woutF", [2048, 1024], BF16)
        b.scr_w_in = b.dram_tmp("scr_w_in", [1024, 4384], BF16)
        b.scr_w_fc = b.dram_tmp("scr_w_fc", [1024, 4096], BF16)
        b.scr_w_proj = b.dram_tmp("scr_w_proj", [4096, 1024], BF16)
    if "l1f" in blocks:
        for nm, sh in [("L1_ffn_ln_w", [1024]), ("L1_ffn_ln_b", [1024]),
                       ("L1_ffn_router_w", [1024, 8]), ("L1_ffn_gate_w", [8, 1024, 2048]),
                       ("L1_ffn_up_w", [8, 1024, 2048]), ("L1_ffn_down_w", [8, 2048, 1024]),
                       ("L1_ffn_sh_gate", [1024, 2048]), ("L1_ffn_sh_up", [1024, 2048]),
                       ("L1_ffn_sh_down", [2048, 1024])]:
            W[nm] = b.dram_in(nm, sh)
        b.scr_gate = b.dram_tmp("scr_gate", [8 * 1024, 2048], BF16)
        b.scr_up = b.dram_tmp("scr_up", [8 * 1024, 2048], BF16)
        b.scr_down = b.dram_tmp("scr_down", [8 * 2048, 1024], BF16)
        b.scr_shg = b.dram_tmp("scr_shg", [1024, 2048], BF16)
        b.scr_shu = b.dram_tmp("scr_shu", [1024, 2048], BF16)
        b.scr_shd = b.dram_tmp("scr_shd", [2048, 1024], BF16)

    emit_consts(b, cx, consts)
    if "l0a" in blocks:
        emit_l0a_setup(b, cx, W)
    if "l0f" in blocks:
        emit_l0f_setup(b, cx, W)
    if "l1a" in blocks:
        emit_l1a_setup(b, cx, W)
    if "l1f" in blocks:
        emit_l1f_setup(b, cx, W)

    ntiles = Lc // T
    for sq in range(seqs):
        for t in range(ntiles):
            b.dma("act", cx.X,
                  xin[sq, t * T:(t + 1) * T, :].rearrange("(s p) d -> p s d", p=128),
                  "ld_x")
            if "l0a" in blocks:
                emit_l0a_tile(b, cx, first=(t == 0), W=W)
            if "l0f" in blocks:
                emit_l0f_tile(b, cx, W)
            if "l1a" in blocks:
                emit_l1a_tile(b, cx, first=(t == 0), W=W)
            if "l1f" in blocks:
                emit_l1f_tile(b, cx, W)
            b.dma("act", xout[sq, t * T:(t + 1) * T, :].rearrange("(s p) d -> p s d", p=128),
                  cx.X, "out_x")
    return b



def _run_launch(blocks, x, winputs, seqs_per_core=2, T=512):
    Lc = x.shape[1]
    b = build_program(blocks, seqs_per_core, Lc, T)
    nc = b.finalize()
    consts = host_consts()
    ncores = x.shape[0] // seqs_per_core
    in_maps = []
    for c in range(ncores):
        m = dict(consts)
        m.update(winputs)
        m["xin"] = np.ascontiguousarray(x[c * seqs_per_core:(c + 1) * seqs_per_core])
        in_maps.append(m)
    res = run_bass_kernel_spmd(nc, in_maps, core_ids=list(range(ncores)))
    return np.concatenate([r["xout"] for r in res.results], axis=0)


def kernel(**inputs):
    x = np.ascontiguousarray(np.asarray(inputs["L0_attn_x"], np.float32))
    w = {k: np.ascontiguousarray(np.asarray(v, np.float32))
         for k, v in inputs.items() if k != "L0_attn_x"}
    return _run_launch(["l0a", "l0f", "l1a", "l1f"], x, w, seqs_per_core=2, T=512)
